# version 02
# baseline (speedup 1.0000x reference)
import numpy as np
import ml_dtypes
import concourse.bass as bass
import concourse.mybir as mybir
from concourse.bass_utils import run_bass_kernel_spmd
from concourse.alu_op_type import AluOpType as OP
from contextlib import ExitStack

F32 = mybir.dt.float32
BF = mybir.dt.bfloat16
AF = mybir.ActivationFunctionType
AX = mybir.AxisListType

S, D, DI, DS, RK, HF, NE, HM, NH, HD = 2048, 1024, 2048, 16, 64, 4096, 8, 2048, 16, 64
TC = 512
NCH = S // TC
DCH = D // 128
ICH = DI // 128
FCH = HF // 128
MCH = HM // 128
B16 = ml_dtypes.bfloat16

_OPTS = {"upto": "full", "taps": ()}

NENG = ("sy", "g", "s", "v", "t")


class Sem:
    def __init__(self, ctx, nc, name):
        self.h = ctx.enter_context(nc.semaphore(name))
        self.n = 0

    def tick(self, k=1):
        self.n += k
        return self.n


class Prog:
    def __init__(self):
        self.ops = {k: [] for k in NENG}

    def add(self, key, fn):
        self.ops[key].append(fn)


class Arena:

    def __init__(self, ctx, nc, name, total_bytes):
        assert total_bytes % 4 == 0
        self.t = ctx.enter_context(
            nc.sbuf_tensor(name, [128, total_bytes // 4], F32))
        self.off = 0
        self.total = total_bytes
        self.peak = 0
        self.name = name

    def _alloc(self, nbytes, align=64):
        o = (self.off + align - 1) // align * align
        self.off = o + nbytes
        self.peak = max(self.peak, self.off)
        assert self.off <= self.total, \
            f"arena {self.name}: {self.off} > {self.total}"
        return o

    def f(self, cols, align=64):
        o = self._alloc(cols * 4, align)
        return self.t[:, o // 4:o // 4 + cols]

    def b(self, cols, align=64):
        o = self._alloc(cols * 2, align)
        return self.t[:, o // 4:o // 4 + (cols + 1) // 2].bitcast(BF)[:, 0:cols]

    def ftiles(self, n, cols):
        return [self.f(cols) for _ in range(n)]

    def btiles(self, n, cols):
        return [self.b(cols) for _ in range(n)]

    def mark(self):
        return self.off

    def release(self, m):
        self.off = m


def _w(sem, val):
    return lambda e: e.wait_ge(sem.h, val)


def _inc(instr, sem, k):
    if sem is not None:
        instr.then_inc(sem.h, k)


def _tick(sem, k):
    if sem is not None:
        sem.tick(k)


def _dma(dst, src, sem):
    _tick(sem, 16)

    def f(e):
        _inc(e.dma_start(out=dst, in_=src), sem, 16)
    return f


def _mm(out, lhsT, rhs, start, stop, sem=None, k=1):
    _tick(sem, k)

    def f(e):
        _inc(e.matmul(out, lhsT, rhs, start=start, stop=stop,
                      skip_group_check=True), sem, k)
    return f


def _trans(out, in_, ident, sem=None, k=1):
    _tick(sem, k)

    def f(e):
        _inc(e.matmul(out, in_, ident, start=True, stop=True,
                      is_transpose=True, skip_group_check=True), sem, k)
    return f


def _act(out, in_, func, bias=None, scale=None, accum=None, sem=None, k=1):
    _tick(sem, k)
    kw = {}
    if bias is not None:
        kw["bias"] = bias
    if scale is not None:
        kw["scale"] = scale
    if accum is not None:
        kw["accum_out"] = accum

    def f(e):
        _inc(e.activation(out=out, in_=in_, func=func, **kw), sem, k)
    return f


def _tt(out, a, b, op, sem=None, k=1):
    _tick(sem, k)

    def f(e):
        _inc(e.tensor_tensor(out=out, in0=a, in1=b, op=op), sem, k)
    return f


def _stt(out, in0, scalar, in1, op0, op1, sem=None, k=1):
    _tick(sem, k)

    def f(e):
        _inc(e.scalar_tensor_tensor(out=out, in0=in0, scalar=scalar, in1=in1,
                                    op0=op0, op1=op1), sem, k)
    return f


def _ts(out, in0, s1, op0, s2=None, op1=None, sem=None, k=1):
    _tick(sem, k)

    def f(e):
        if op1 is None:
            _inc(e.tensor_scalar(out=out, in0=in0, scalar1=s1, scalar2=None,
                                 op0=op0), sem, k)
        else:
            _inc(e.tensor_scalar(out=out, in0=in0, scalar1=s1, scalar2=s2,
                                 op0=op0, op1=op1), sem, k)
    return f


def _scan(out, d0, d1, init, sem=None, k=1):
    _tick(sem, k)

    def f(e):
        _inc(e.tensor_tensor_scan(out=out, data0=d0, data1=d1, initial=init,
                                  op0=OP.mult, op1=OP.add), sem, k)
    return f


def _vcopy(out, in_, sem=None, k=1):
    _tick(sem, k)

    def f(e):
        _inc(e.tensor_copy(out, in_), sem, k)
    return f


def _memset(ap, v, sem=None, k=1):
    _tick(sem, k)

    def f(e):
        _inc(e.memset(ap, v), sem, k)
    return f


def _red(out, in_, op, sem=None, k=1):
    _tick(sem, k)

    def f(e):
        _inc(e.tensor_reduce(out=out, in_=in_, axis=AX.X, op=op), sem, k)
    return f


def _max8(out, in_, sem=None, k=1):
    _tick(sem, k)

    def f(e):
        _inc(e.max(out=out, in_=in_), sem, k)
    return f


def _recip(out, in_, sem=None, k=1):
    _tick(sem, k)

    def f(e):
        _inc(e.reciprocal(out=out, in_=in_), sem, k)
    return f


def _ap3(src, d0, d1):
    return bass.AP(src.tensor, src.offset, [src.ap[0], list(d0), list(d1)])


class DmaQ:

    def __init__(self, ctx, nc, P, qkey, name, k=4):
        self.P, self.qkey = P, qkey
        self.sems = [Sem(ctx, nc, f"{name}{i}") for i in range(k)]
        self.k = k
        self.i = 0

    def dma(self, dst, src, extra_wait=None):
        sm = self.sems[self.i % self.k]
        self.i += 1
        if extra_wait is not None:
            self.P.add(self.qkey, _w(*extra_wait))
        if sm.n:
            self.P.add(self.qkey, _w(sm, sm.n))
        self.P.add(self.qkey, _dma(dst, src, sm))
        return (sm, sm.n)

    def drain(self):
        for sm in self.sems:
            if sm.n:
                self.P.add(self.qkey, _w(sm, sm.n))


class WPool:

    def __init__(self, q, ar, cols, depth):
        self.q = q
        self.bf = [ar.b(cols) for _ in range(depth)]
        self.depth = depth
        self.bf_free = [None] * depth
        self.i = 0

    def load(self, src_ap, ncols, inner=None):
        sl = self.i % self.depth
        self.i += 1
        bf = self.bf[sl][:, 0:ncols]
        dst = bf if inner is None else bf.rearrange("p (k c) -> p k c", c=inner)
        tk = self.q.dma(dst, src_ap, extra_wait=self.bf_free[sl])
        return sl, bf, tk

    def consumed(self, sl, sem, val):
        self.bf_free[sl] = (sem, val)


class Banks:
    def __init__(self, P):
        self.P = P
        self.free = {}

    def twait(self, b):
        if self.free.get(b):
            self.P.add("t", _w(*self.free[b]))

    def set_free(self, b, sem, val):
        self.free[b] = (sem, val)


def build(upto="full", taps=()):
    nc = bass.Bass("TRN2", target_bir_lowering=False,
                   detect_race_conditions=False)

    def din(name, shape, dtype=F32):
        return nc.dram_tensor(name, shape, dtype, kind="ExternalInput").ap()

    def dout(name, shape, dtype=F32):
        return nc.dram_tensor(name, shape, dtype, kind="ExternalOutput").ap()

    def dint(name, shape, dtype=F32):
        kind = "ExternalOutput" if name in taps else "Internal"
        return nc.dram_tensor(name, shape, dtype, kind=kind).ap()

    x0T = din("x0T", [D, S])
    w_inT = din("w_inT", [D, 2 * DI])
    convw = din("convw", [128, 64])
    convb = din("convb", [128, 16])
    xprojT = din("xprojT", [DI, 128])
    dtprojT = din("dtprojT", [RK, DI])
    dtbd = din("dtbd", [128, 16])
    alogd = din("alogd", [128, 256])
    ddev = din("ddev", [128, 16])
    outpT = din("outpT", [DI, D])
    f1T = din("f1T", [D, HF])
    fb1d = din("fb1d", [128, 32])
    f2T = din("f2T", [HF, D])
    fb2d = din("fb2d", [128, 8])
    n1wd = din("n1wd", [128, 8])
    n2wd = din("n2wd", [128, 8])
    qkvT = din("qkvT", [D, 3 * D])
    qkvbd = din("qkvbd", [128, 24])
    bvrow = din("bvrow", [1, D])
    outwT = din("outwT", [D, D])
    outbd = din("outbd", [128, 8])
    gwT = din("gwT", [D, NE])
    w1T = din("w1T", [NE, D, HM])
    w2T = din("w2T", [NE, D, HM])
    w3T = din("w3T", [NE, HM, D])
    b1d = din("b1d", [128, 128])
    b2d = din("b2d", [128, 128])
    b3d = din("b3d", [128, 64])
    identb = din("identb", [128, 128], BF)
    onesb = din("onesb", [128, 128], BF)
    rotPd = din("rotP", [128, 128], BF)
    cosTd = din("cosT", [128, S], BF)
    sinTd = din("sinT", [128, S], BF)
    selbd = din("selb", [16, 2048], BF)
    hbqd = din("hbq", [128, 32], BF)

    x1sp = dint("x1sp", [D, S])
    z0sp = dint("z0sp", [D, S])
    c_sp = dint("c_sp", [D, S], BF)
    atsp = dint("atsp", [D, S], BF)
    z1sp = dint("z1sp", [D, S])
    x2T = dout("x2T", [D, S])

    tapw = {}

    def tap(name, shape, dtype=F32):
        if name in taps:
            tapw[name] = nc.dram_tensor("dbg_" + name, shape, dtype,
                                        kind="ExternalOutput").ap()
            return tapw[name]
        return None

    P = Prog()
    with ExitStack() as ctx:
        sT = Sem(ctx, nc, "sT")
        sV = Sem(ctx, nc, "sV")
        sS = Sem(ctx, nc, "sS")
        qW = DmaQ(ctx, nc, P, "sy", "qW", 4)
        qG = DmaQ(ctx, nc, P, "g", "qG", 4)
        sGB = Sem(ctx, nc, "sGB")

        AR = Arena(ctx, nc, "arena", 178 * 1024)

        ps_h = [ctx.enter_context(nc.psum_tensor(f"ps{i}", [128, 512], F32))
                for i in range(7)]
        ps = [h[:] for h in ps_h]
        psbf = ctx.enter_context(nc.psum_tensor("psbf", [128, 1024], BF))[:]
        banks = Banks(P)

        def psb(name, shape, dtype):
            return ctx.enter_context(nc.sbuf_tensor(name, shape, dtype))

        c_ident = psb("c_ident", [128, 128], BF)
        c_ones = psb("c_ones", [128, 128], BF)
        c_rotP = psb("c_rotP", [128, 128], BF)
        c_convw = psb("c_convw", [128, 64], F32)
        c_convb = psb("c_convb", [128, 16], F32)
        c_dtb = psb("c_dtb", [128, 16], F32)
        c_A = psb("c_A", [128, 256], F32)
        c_D = psb("c_D", [128, 16], F32)
        c_fb1 = psb("c_fb1", [128, 32], F32)
        c_fb2 = psb("c_fb2", [128, 8], F32)
        c_n1w = psb("c_n1w", [128, 8], F32)
        c_n2w = psb("c_n2w", [128, 8], F32)
        c_qkvb = psb("c_qkvb", [128, 24], F32)
        c_bv = psb("c_bv", [1, D], F32)
        c_outb = psb("c_outb", [128, 8], F32)
        c_b1 = psb("c_b1", [128, 128], F32)
        c_b2 = psb("c_b2", [128, 128], F32)
        c_b3 = psb("c_b3", [128, 64], F32)
        c_xpj = psb("c_xpj", [128, 16 * 128], BF)
        c_dtp = psb("c_dtp", [RK, DI], BF)
        c_gw = psb("c_gw", [128, 64], BF)
        c_sel = psb("c_sel", [16, 2048], BF)
        c_hbq = psb("c_hbq", [128, 32], BF)
        hcar = psb("hcar", [128, 256], F32)
        sums = psb("sums", [128, S], F32)
        srow = psb("srow", [1, 32], F32)
        xm_tail = psb("xm_tail", [128, 48], BF)
        c_eps = psb("c_eps", [128, 2], F32)

        sumc = sums[0:1, :]
        sumc2 = sums[32:33, :]
        sumx = sums[64:65, :]
        sumx2 = sums[96:97, :]
        ones_col = c_ones[:, 0:1]
        ones_row = c_ones[0:1, :]

        for dst, src in [(c_ident, identb), (c_ones, onesb), (c_rotP, rotPd),
                         (c_sel, selbd), (c_hbq, hbqd),
                         (c_convw, convw), (c_convb, convb), (c_dtb, dtbd),
                         (c_D, ddev), (c_fb1, fb1d), (c_fb2, fb2d),
                         (c_n1w, n1wd), (c_n2w, n2wd), (c_qkvb, qkvbd),
                         (c_bv, bvrow), (c_outb, outbd), (c_b1, b1d),
                         (c_b2, b2d), (c_b3, b3d)]:
            qW.dma(dst[:], src)
        aw_tk = qW.dma(c_A[:], alogd)
        P.add("s", _w(*aw_tk))
        P.add("s", _act(c_A[:], c_A[:], AF.Exp, sem=sS))
        P.add("v", _w(sS, sS.n))
        P.add("v", _ts(c_A[:], c_A[:], -1.0, OP.mult, sem=sV))
        a_ready = sV.n
        m0 = AR.mark()
        tmpst = AR.f(2048)
        for dst, src, rows, cols, inner in [
            (c_xpj, xprojT.rearrange("(j p) c -> p j c", p=128), 128, 16 * 128, 128),
            (c_dtp, dtprojT, RK, DI, None),
            (c_gw, gwT.rearrange("(j p) c -> p j c", p=128), 128, 64, 8),
        ]:
            dd = tmpst[0:rows, 0:cols]
            if inner is not None:
                dd = dd.rearrange("p (j c) -> p j c", c=inner)
            tk = qW.dma(dd, src)
            P.add("s", _w(*tk))
            P.add("s", _act(dst[0:rows, 0:cols], tmpst[0:rows, 0:cols],
                            AF.Copy, sem=sS))
            P.add("sy", _w(sS, sS.n))
        AR.release(m0)
        P.add("v", _memset(hcar[:], 0.0))
        P.add("v", _memset(c_eps[:, 0:1], 1e-5))
        P.add("v", _memset(c_eps[:, 1:2], 1e-6))
        P.add("v", _memset(sums[:], 0.0, sem=sV))
        init_v = sV.n

        def chunk_barrier():
            tv, ts_, tt = sV.n, sS.n, sT.n
            qG.drain()
            gb = sGB.tick()
            P.add("g", _w(sV, tv))
            P.add("g", _w(sS, ts_))
            P.add("g", _w(sT, tt))
            P.add("g", lambda e, h=sGB.h: e.engine_nop().then_inc(h, 1))
            P.add("v", _w(sS, ts_))
            P.add("v", _w(sT, tt))
            P.add("v", _w(sGB, gb))
            P.add("s", _w(sV, tv))
            P.add("s", _w(sT, tt))
            P.add("s", _w(sGB, gb))
            P.add("t", _w(sV, tv))
            P.add("t", _w(sS, ts_))
            P.add("t", _w(sGB, gb))

        def replicate_row(src_row_bf, dst_f32, bank, bp=0):
            n = src_row_bf.ap[-1][1]
            nsl = (n + 511) // 512
            orow = c_ones[bp:bp + 1, :]
            for i in range(nsl):
                c0, c1 = i * 512, min(n, (i + 1) * 512)
                banks.twait(bank)
                P.add("t", _mm(ps[bank][0:128, 0:c1 - c0], orow,
                               src_row_bf[:, c0:c1], True, True, sem=sT))
                P.add("s", _w(sT, sT.n))
                P.add("s", _act(dst_f32[:, c0:c1], ps[bank][0:128, 0:c1 - c0],
                                AF.Copy, sem=sS))
                banks.set_free(bank, sS, sS.n)

        wpool = WPool(qG, AR, 2048, 4)
        p1mark = AR.mark()

        n0 = AR.btiles(DCH, TC)
        res_s = AR.btiles(ICH, TC)
        u_bf = AR.btiles(ICH, TC)
        delta = AR.btiles(ICH, TC)
        dt_bf = AR.b(TC)
        B_bf = AR.b(TC)
        C_bf = AR.b(TC)
        B_bc = [AR.b(8 * TC) for _ in range(2)]
        C_bc = [AR.b(8 * TC) for _ in range(2)]
        y2 = AR.btiles(ICH, TC)

        t_u = tap("u", [DI, S])
        t_delta = tap("delta", [DI, S])
        t_y2 = tap("y2", [DI, S])
        t_m = tap("m", [D, S])
        t_n0 = tap("n0", [D, S])

        P.add("t", _w(sV, init_v))
        P.add("t", _w(sV, a_ready))

        bank_rr = {"i": 0}

        def next_pair():
            p = bank_rr["i"] % 3
            bank_rr["i"] += 1
            return (2 * p, 2 * p + 1)

        regmark = AR.mark()

        for c in range(NCH):
            csl = slice(c * TC, (c + 1) * TC)
            chunk_barrier()
            AR.release(regmark)
            x0c = AR.f(DCH * TC)
            x0cT = [x0c[:, m * TC:(m + 1) * TC] for m in range(DCH)]
            shad8 = AR.btiles(DCH, TC)
            shsq8 = AR.btiles(DCH, TC)
            rowb0 = AR.b(TC)
            rowb1 = AR.b(TC)
            mu_row = AR.f(TC)[0:1, :]
            r_row = AR.f(TC)[0:1, :]
            v_scr = AR.f(TC)[0:1, :]
            murep = AR.f(TC)
            rrep = AR.f(TC)

            gx_tk = qG.dma(x0c.rearrange("p (m t) -> p m t", m=DCH),
                           x0T[:, csl].rearrange("(m p) t -> p m t", p=128))
            P.add("s", _w(*gx_tk))
            for m in range(DCH):
                P.add("s", _act(shad8[m], x0cT[m], AF.Copy))
                P.add("s", _act(shsq8[m], x0cT[m], AF.Square,
                                sem=sS if m == DCH - 1 else None))
            ct = sS.n
            banks.twait(4)
            banks.twait(5)
            P.add("t", _w(sS, ct))
            for m in range(DCH):
                P.add("t", _mm(ps[4][0:1, :], ones_col, shad8[m],
                               m == 0, m == DCH - 1))
                P.add("t", _mm(ps[5][0:1, :], ones_col, shsq8[m],
                               m == 0, m == DCH - 1,
                               sem=sT if m == DCH - 1 else None))
            P.add("v", _w(sT, sT.n))
            P.add("v", _ts(mu_row, ps[4][0:1, :], 1.0 / D, OP.mult))
            P.add("v", _ts(r_row, ps[5][0:1, :], 1.0 / D, OP.mult, sem=sV))
            banks.set_free(4, sV, sV.n)
            banks.set_free(5, sV, sV.n)
            P.add("v", _tt(v_scr, mu_row, mu_row, OP.mult))
            P.add("v", _tt(r_row, r_row, v_scr, OP.subtract, sem=sV))
            P.add("s", _w(sV, sV.n))
            P.add("s", _act(r_row, r_row, AF.Sqrt, bias=c_eps[0:1, 0:1], sem=sS))
            P.add("v", _w(sS, sS.n))
            P.add("v", _recip(r_row, r_row))
            P.add("v", _tt(v_scr, mu_row, mu_row, OP.mult, sem=sV))
            P.add("s", _w(sV, sV.n))
            P.add("s", _act(rowb0[0:1, :], mu_row, AF.Copy))
            P.add("s", _act(rowb1[0:1, :], r_row, AF.Copy, sem=sS))
            P.add("t", _w(sS, sS.n))
            replicate_row(rowb0[0:1, :], murep, 4)
            replicate_row(rowb1[0:1, :], rrep, 5)
            P.add("v", _w(sS, sS.n))
            P.add("v", _w(*gx_tk))
            for m in range(DCH):
                P.add("v", _tt(x0cT[m], x0cT[m], murep, OP.subtract))
                P.add("v", _tt(n0[m], x0cT[m], rrep, OP.mult,
                               sem=sV if m == DCH - 1 else None))
            n0t = sV.n
            if t_n0 is not None:
                P.add("g", _w(sV, n0t))
                for m in range(DCH):
                    qG.dma(t_n0[m * 128:(m + 1) * 128, csl], n0[m])

            AR.release(regmark)
            xm_pad = AR.btiles(ICH, TC + 3)
            cvA = [AR.f(TC) for _ in range(2)]
            cvV = [AR.f(TC) for _ in range(2)]

            if c == 0:
                for j in range(ICH):
                    P.add("v", _memset(xm_pad[j][:, 0:3], 0.0,
                                       sem=sV if j == ICH - 1 else None))
            else:
                for j in range(ICH):
                    P.add("v", _vcopy(xm_pad[j][:, 0:3],
                                      xm_tail[:, j * 3:(j + 1) * 3],
                                      sem=sV if j == ICH - 1 else None))
            tail_t = sV.n

            first_evac = True
            for mg in range(16):
                src = w_inT[:, mg * 256:(mg + 1) * 256] \
                    .rearrange("(k p) c -> p k c", p=128)
                sl, wbf, (csem, cval) = wpool.load(src, 2048, inner=256)
                b0, b1_ = next_pair()
                banks.twait(b0)
                banks.twait(b1_)
                P.add("t", _w(csem, cval))
                P.add("t", _w(sV, n0t))
                P.add("t", _w(sV, tail_t))
                for k in range(DCH):
                    P.add("t", _mm(ps[b0], wbf[:, k * 256:k * 256 + 128],
                                   n0[k], k == 0, k == DCH - 1))
                    P.add("t", _mm(ps[b1_], wbf[:, k * 256 + 128:k * 256 + 256],
                                   n0[k], k == 0, k == DCH - 1,
                                   sem=sT if k == DCH - 1 else None))
                tt = sT.n
                wpool.consumed(sl, sT, tt)
                P.add("s", _w(sT, tt))
                if first_evac:
                    P.add("s", _w(sV, tail_t))
                    first_evac = False
                for pi, mi in ((b0, 2 * mg), (b1_, 2 * mg + 1)):
                    if mi < ICH:
                        P.add("s", _act(xm_pad[mi][:, 3:3 + TC], ps[pi],
                                        AF.Copy, sem=sS))
                    else:
                        P.add("s", _act(res_s[mi - ICH], ps[pi], AF.Silu,
                                        sem=sS))
                banks.set_free(b0, sS, sS.n)
                banks.set_free(b1_, sS, sS.n)
            inproj_t = sS.n

            for j in range(ICH):
                w4 = [c_convw[:, j * 4 + t:j * 4 + t + 1] for t in range(4)]
                if j == 0:
                    P.add("s", _w(sS, inproj_t))
                P.add("s", _act(cvA[0], xm_pad[j][:, 0:TC], AF.Copy,
                                scale=w4[0]))
                P.add("s", _act(cvA[1], xm_pad[j][:, 1:1 + TC], AF.Copy,
                                scale=w4[1], sem=sS))
                at = sS.n
                P.add("v", _w(sS, at))
                P.add("v", _stt(cvV[0], xm_pad[j][:, 2:2 + TC], w4[2],
                                cvA[0], OP.mult, OP.add))
                P.add("v", _stt(cvV[1], xm_pad[j][:, 3:3 + TC], w4[3],
                                cvA[1], OP.mult, OP.add))
                P.add("v", _vcopy(xm_tail[:, j * 3:(j + 1) * 3],
                                  xm_pad[j][:, 512:515]))
                P.add("v", _tt(cvV[0], cvV[0], cvV[1], OP.add, sem=sV))
                vt = sV.n
                P.add("s", _w(sV, vt))
                P.add("s", _act(u_bf[j], cvV[0], AF.Silu,
                                bias=c_convb[:, j:j + 1], sem=sS))
            u_t = sS.n
            if t_u is not None:
                P.add("g", _w(sS, u_t))
                for j in range(ICH):
                    qG.dma(t_u[j * 128:(j + 1) * 128, csl], u_bf[j])

            AR.release(regmark)
            du_b = AR.b(TC)
            dBu = AR.b(8 * TC)
            h8 = AR.b(8 * TC)
            hCb = AR.b(8 * TC)
            tr1 = h8[:, 0:4 * TC]
            tr2 = hCb[:, 0:2 * TC]
            dA2 = [AR.f(TC) for _ in range(2)]
            dA_free = [0, 0]
            sc4 = [AR.f(TC) for _ in range(4)]
            sc_free = [0, 0]
            ygr = [AR.f(TC) for _ in range(2)]
            yfull = AR.f(TC)

            banks.twait(4)
            P.add("t", _w(sS, u_t))
            for j in range(ICH):
                P.add("t", _mm(ps[4], c_xpj[:, j * 128:(j + 1) * 128],
                               u_bf[j], j == 0, j == ICH - 1,
                               sem=sT if j == ICH - 1 else None))
            P.add("s", _w(sT, sT.n))
            P.add("s", _act(dt_bf[0:64, :], ps[4][0:64, :], AF.Copy))
            P.add("s", _act(B_bf[0:16, :], ps[4][64:80, :], AF.Copy))
            P.add("s", _act(C_bf[0:16, :], ps[4][96:112, :], AF.Copy, sem=sS))
            banks.set_free(4, sS, sS.n)
            xdbl_t = sS.n

            for m in range(ICH):
                b0 = 4 + (m % 2)
                banks.twait(b0)
                if m == 0:
                    P.add("t", _w(sS, xdbl_t))
                P.add("t", _mm(ps[b0], c_dtp[:, m * 128:(m + 1) * 128],
                               dt_bf[0:64, :], True, True, sem=sT))
                dcol = c_dtb[:, m:m + 1]
                pp = m % 2
                s0, s1 = sc4[2 * pp], sc4[2 * pp + 1]
                P.add("s", _w(sT, sT.n))
                if sc_free[pp]:
                    P.add("s", _w(sV, sc_free[pp]))
                P.add("s", _act(s0, ps[b0], AF.Abs, bias=dcol))
                P.add("s", _act(s1, s0, AF.Exp, scale=-1.0))
                P.add("s", _act(s0, s1, AF.Ln, bias=1.0))
                P.add("s", _act(s1, ps[b0], AF.Relu, bias=dcol, sem=sS))
                st_ = sS.n
                banks.set_free(b0, sS, st_)
                P.add("v", _w(sS, st_))
                P.add("v", _tt(delta[m], s0, s1, OP.add, sem=sV))
                sc_free[pp] = sV.n
            delta_t = sV.n
            if t_delta is not None:
                P.add("g", _w(sV, delta_t))
                for j in range(ICH):
                    qG.dma(t_delta[j * 128:(j + 1) * 128, csl],
                                    delta[j])

            for g in range(2):
                for srcrow, dstbc in ((B_bf, B_bc[g]), (C_bf, C_bc[g])):
                    for i in range(8):
                        n = g * 8 + i
                        bnk = 4 + (i % 2)
                        banks.twait(bnk)
                        if g == 0 and i == 0 and srcrow is B_bf:
                            P.add("t", _w(sS, xdbl_t))
                        P.add("t", _mm(ps[bnk],
                                       c_sel[:, n * 128:(n + 1) * 128],
                                       srcrow[0:16, 0:TC], True, True,
                                       sem=sT))
                        P.add("s", _w(sT, sT.n))
                        P.add("s", _act(dstbc[:, i * TC:(i + 1) * TC],
                                        ps[bnk], AF.Copy, sem=sS))
                        banks.set_free(bnk, sS, sS.n)
            bc_t = sS.n

            for j in range(ICH):
                P.add("v", _tt(du_b, delta[j], u_bf[j], OP.mult))
                for g in range(2):
                    du3 = _ap3(du_b, (0, 8), (1, TC))
                    bb3 = _ap3(B_bc[g], (TC, 8), (1, TC))
                    if j == 0 and g == 0:
                        P.add("v", _w(sS, bc_t))
                    P.add("v", _tt(_ap3(dBu, (TC, 8), (1, TC)), du3, bb3,
                                   OP.mult))
                    for i in range(8):
                        n = g * 8 + i
                        si = n % 2
                        if j == 0 and g == 0 and i == 0:
                            P.add("s", _w(sS, bc_t))
                        if dA_free[si]:
                            P.add("s", _w(sV, dA_free[si]))
                        P.add("s", _act(dA2[si], delta[j], AF.Exp,
                                        scale=c_A[:, j * 16 + n:j * 16 + n + 1],
                                        sem=sS))
                        P.add("v", _w(sS, sS.n))
                        P.add("v", _scan(h8[:, i * TC:(i + 1) * TC], dA2[si],
                                         dBu[:, i * TC:(i + 1) * TC],
                                         hcar[:, j * 16 + n:j * 16 + n + 1],
                                         sem=sV))
                        dA_free[si] = sV.n
                    hlast = bass.AP(h8.tensor, h8.offset + (TC - 1) * 1,
                                    [h8.ap[0], [TC, 8]])
                    P.add("v", _vcopy(
                        hcar[:, j * 16 + g * 8:j * 16 + g * 8 + 8], hlast))
                    P.add("v", _tt(hCb, h8, C_bc[g], OP.mult))
                    P.add("v", _tt(tr1, hCb[:, 0:4 * TC], hCb[:, 4 * TC:8 * TC],
                                   OP.add))
                    P.add("v", _tt(tr2, tr1[:, 0:2 * TC], tr1[:, 2 * TC:4 * TC],
                                   OP.add))
                    P.add("v", _tt(ygr[g], tr2[:, 0:TC], tr2[:, TC:2 * TC],
                                   OP.add))
                P.add("v", _stt(yfull, u_bf[j], c_D[:, j:j + 1], ygr[0],
                                OP.mult, OP.add))
                P.add("v", _tt(yfull, yfull, ygr[1], OP.add))
                P.add("v", _tt(y2[j], yfull, res_s[j], OP.mult,
                               sem=sV if j == ICH - 1 else None))
            y2_t = sV.n
            if t_y2 is not None:
                P.add("g", _w(sV, y2_t))
                for j in range(ICH):
                    qG.dma(t_y2[j * 128:(j + 1) * 128, csl], y2[j])

            AR.release(regmark)
            m_bf = AR.btiles(DCH, TC)
            g_bf = AR.btiles(FCH, TC)
            cch = AR.btiles(DCH, TC)
            c2s = AR.btiles(2, TC)
            c2_free = [0, 0]

            for m in range(DCH):
                src = outpT[:, m * 128:(m + 1) * 128] \
                    .rearrange("(k p) c -> p k c", p=128)
                sl, wbf, (csem, cval) = wpool.load(src, 2048, inner=128)
                b0 = (m % 2) * 2
                banks.twait(b0)
                P.add("t", _w(csem, cval))
                P.add("t", _w(sV, y2_t))
                for k in range(ICH):
                    P.add("t", _mm(ps[b0], wbf[:, k * 128:(k + 1) * 128],
                                   y2[k], k == 0, k == ICH - 1,
                                   sem=sT if k == ICH - 1 else None))
                tt = sT.n
                wpool.consumed(sl, sT, tt)
                P.add("s", _w(sT, tt))
                P.add("s", _act(m_bf[m], ps[b0], AF.Copy, sem=sS))
                banks.set_free(b0, sS, sS.n)
            m_t = sS.n
            if t_m is not None:
                P.add("g", _w(sS, m_t))
                for m in range(DCH):
                    qG.dma(t_m[m * 128:(m + 1) * 128, csl], m_bf[m])

            for mg in range(16):
                src = f1T[:, mg * 256:(mg + 1) * 256] \
                    .rearrange("(k p) c -> p k c", p=128)
                sl, wbf, (csem, cval) = wpool.load(src, 2048, inner=256)
                b0, b1_ = next_pair()
                banks.twait(b0)
                banks.twait(b1_)
                P.add("t", _w(csem, cval))
                P.add("t", _w(sS, m_t))
                for k in range(DCH):
                    P.add("t", _mm(ps[b0], wbf[:, k * 256:k * 256 + 128],
                                   m_bf[k], k == 0, k == DCH - 1))
                    P.add("t", _mm(ps[b1_], wbf[:, k * 256 + 128:k * 256 + 256],
                                   m_bf[k], k == 0, k == DCH - 1,
                                   sem=sT if k == DCH - 1 else None))
                tt = sT.n
                wpool.consumed(sl, sT, tt)
                P.add("s", _w(sT, tt))
                for pi, mi in ((b0, 2 * mg), (b1_, 2 * mg + 1)):
                    P.add("s", _act(g_bf[mi], ps[pi], AF.Gelu,
                                    bias=c_fb1[:, mi:mi + 1], sem=sS))
                banks.set_free(b0, sS, sS.n)
                banks.set_free(b1_, sS, sS.n)
            gg_t = sS.n

            for m in range(DCH):
                sls = []
                for hh in range(2):
                    src = f2T[hh * 2048:(hh + 1) * 2048, m * 128:(m + 1) * 128] \
                        .rearrange("(k p) c -> p k c", p=128)
                    sls.append(wpool.load(src, 2048, inner=128))
                b0 = (m % 2) * 2 + 1
                banks.twait(b0)
                P.add("t", _w(sls[0][2][0], sls[0][2][1]))
                P.add("t", _w(sls[1][2][0], sls[1][2][1]))
                P.add("t", _w(sS, gg_t))
                for k in range(FCH):
                    wbf = sls[k // 16][1]
                    P.add("t", _mm(ps[b0],
                                   wbf[:, (k % 16) * 128:(k % 16 + 1) * 128],
                                   g_bf[k], k == 0, k == FCH - 1,
                                   sem=sT if k == FCH - 1 else None))
                tt = sT.n
                for hh in range(2):
                    wpool.consumed(sls[hh][0], sT, tt)
                P.add("v", _w(sT, tt))
                P.add("v", _stt(cch[m], ps[b0], c_fb2[:, m:m + 1],
                                m_bf[m], OP.add, OP.add, sem=sV))
                banks.set_free(b0, sV, sV.n)
                cvt = sV.n
                qG.dma(c_sp[m * 128:(m + 1) * 128, csl], cch[m],
                       extra_wait=(sV, cvt))
                if c2_free[m % 2]:
                    P.add("s", _w(sT, c2_free[m % 2]))
                P.add("s", _w(sV, cvt))
                P.add("s", _act(c2s[m % 2], cch[m], AF.Square, sem=sS))
                ct = sS.n
                if m == 0:
                    banks.twait(4)
                    banks.twait(5)
                P.add("t", _w(sS, ct))
                P.add("t", _mm(ps[4][0:1, :], ones_col, cch[m],
                               m == 0, m == DCH - 1))
                P.add("t", _mm(ps[5][0:1, :], ones_col, c2s[m % 2],
                               m == 0, m == DCH - 1, sem=sT))
                c2_free[m % 2] = sT.n
            P.add("v", _w(sT, sT.n))
            P.add("v", _vcopy(sumc[:, csl], ps[4][0:1, :]))
            P.add("v", _vcopy(sumc2[:, csl], ps[5][0:1, :], sem=sV))
            banks.set_free(4, sV, sV.n)
            banks.set_free(5, sV, sV.n)

        chunk_barrier()
        AR.release(p1mark)
        qG.drain()
        n1z = AR.btiles(DCH, S)
        zmark = AR.mark()
        z0 = AR.ftiles(DCH, S)
        p1cm = AR.mark()

        P.add("v", _red(srow[0:1, 0:1], sumc, OP.add))
        P.add("v", _red(srow[0:1, 1:2], sumc2, OP.add))
        P.add("v", _memset(srow[0:1, 8:9], 0.0))
        P.add("v", _ts(srow[0:1, 0:1], srow[0:1, 0:1], 1.0 / (S * D), OP.mult))
        P.add("v", _ts(srow[0:1, 1:2], srow[0:1, 1:2], 1.0 / (S * D), OP.mult))
        P.add("v", _tt(srow[0:1, 2:3], srow[0:1, 0:1], srow[0:1, 0:1],
                       OP.mult))
        P.add("v", _tt(srow[0:1, 3:4], srow[0:1, 1:2], srow[0:1, 2:3],
                       OP.subtract))
        P.add("v", _ts(srow[0:1, 3:4], srow[0:1, 3:4], 1e-5, OP.add))
        P.add("v", _ts(srow[0:1, 3:4], srow[0:1, 3:4], 1e-5, OP.mult, sem=sV))

        rowsF = AR.f(S)
        mtok = rowsF[0:1, 0:S]
        vtok = rowsF[32:33, 0:S]
        rtok = rowsF[64:65, 0:S]
        mrepS = AR.f(S)
        rrepS = AR.f(S)
        q2r = mrepS[32:33, 0:S]
        epsY = rrepS[32:33, 0:1]
        P.add("v", _ts(mtok, sumc, 1.0 / D, OP.mult))
        P.add("v", _ts(vtok, sumc2, 1.0 / D, OP.mult))
        P.add("v", _tt(q2r, mtok, mtok, OP.mult))
        P.add("v", _tt(vtok, vtok, q2r, OP.subtract, sem=sV))
        P.add("s", _w(sV, sV.n))
        P.add("s", _act(epsY, srow[0:1, 3:4], AF.Copy, sem=sS))
        P.add("v", _w(sS, sS.n))
        P.add("v", _ts(vtok, vtok, epsY, OP.add, sem=sV))
        P.add("s", _w(sV, sV.n))
        P.add("s", _act(vtok, vtok, AF.Sqrt, sem=sS))
        P.add("v", _w(sS, sS.n))
        P.add("v", _recip(rtok, vtok))
        P.add("v", _ts(vtok, vtok, 1.0, OP.mult, sem=sV))
        tok_t = sV.n

        rowsB = AR.b(S)
        rows2 = [rowsB[0:1, :], rowsB[32:33, :]]
        P.add("s", _w(sV, tok_t))
        P.add("s", _act(rows2[0], mtok, AF.Copy))
        P.add("s", _act(rows2[1], rtok, AF.Copy, sem=sS))
        P.add("t", _w(sS, sS.n))
        replicate_row(rows2[0], mrepS, 4)
        replicate_row(rows2[1], rrepS, 5, bp=32)
        rep_t = sS.n

        x0r = AR.f(S)
        x1b = AR.f(S)
        t1b = AR.f(S)
        crel = AR.b(S)
        sh2 = [AR.b(S) for _ in range(2)]
        P.add("v", _w(sS, rep_t))
        prev_v = 0
        for m in range(DCH):
            ew = (sV, prev_v) if prev_v else None
            t1_tk = qG.dma(x0r, x0T[m * 128:(m + 1) * 128, :], extra_wait=ew)
            t2_tk = qG.dma(crel, c_sp[m * 128:(m + 1) * 128, :])
            P.add("v", _w(*t1_tk))
            P.add("v", _w(*t2_tk))
            P.add("v", _tt(t1b, crel, mrepS, OP.subtract))
            P.add("v", _tt(t1b, t1b, rrepS, OP.mult))
            P.add("v", _tt(x1b, x0r, t1b, OP.add, sem=sV))
            vt = sV.n
            sp_tk = qG.dma(x1sp[m * 128:(m + 1) * 128, :], x1b,
                           extra_wait=(sV, vt))
            P.add("s", _w(sV, vt))
            P.add("s", _act(sh2[0], x1b, AF.Copy))
            P.add("s", _act(sh2[1], x1b, AF.Square, sem=sS))
            ct = sS.n
            banks.twait(4)
            banks.twait(5)
            P.add("t", _w(sS, ct))
            for i in range(4):
                isl = slice(i * 512, (i + 1) * 512)
                if i > 0:
                    P.add("t", _w(sV, sV.n))
                P.add("t", _mm(ps[4][0:1, :], ones_col, sh2[0][:, isl],
                               True, True))
                P.add("t", _mm(ps[5][0:1, :], ones_col, sh2[1][:, isl],
                               True, True, sem=sT))
                P.add("v", _w(sT, sT.n))
                P.add("v", _tt(sumx[:, isl], sumx[:, isl], ps[4][0:1, :],
                               OP.add))
                P.add("v", _tt(sumx2[:, isl], sumx2[:, isl],
                               ps[5][0:1, :], OP.add, sem=sV))
            banks.set_free(4, sV, sV.n)
            banks.set_free(5, sV, sV.n)
            P.add("v", _w(*sp_tk))
            prev_v = sV.n
        x1_t = prev_v

        P.add("v", _ts(mtok, sumx, 1.0 / D, OP.mult))
        P.add("v", _ts(vtok, sumx2, 1.0 / D, OP.mult))
        P.add("v", _tt(q2r, mtok, mtok, OP.mult))
        P.add("v", _tt(vtok, vtok, q2r, OP.subtract, sem=sV))
        P.add("s", _w(sV, sV.n))
        P.add("s", _act(vtok, vtok, AF.Sqrt, bias=c_eps[32:33, 0:1], sem=sS))
        P.add("v", _w(sS, sS.n))
        P.add("v", _recip(rtok, vtok))
        P.add("v", _ts(vtok, vtok, 1.0, OP.mult, sem=sV))
        P.add("s", _w(sV, sV.n))
        P.add("s", _act(rows2[0], mtok, AF.Copy))
        P.add("s", _act(rows2[1], rtok, AF.Copy, sem=sS))
        P.add("t", _w(sS, sS.n))
        replicate_row(rows2[0], mrepS, 4)
        replicate_row(rows2[1], rrepS, 5, bp=32)
        rep2_t = sS.n

        x1r2 = x0r
        t_z0 = tap("z0", [D, S])
        prev_v = 0
        for m in range(DCH):
            ew = (sV, x1_t if not prev_v else prev_v)
            r_tk = qG.dma(x1r2, x1sp[m * 128:(m + 1) * 128, :], extra_wait=ew)
            P.add("v", _w(*r_tk))
            if m == 0:
                P.add("v", _w(sS, rep2_t))
            P.add("v", _tt(z0[m], x1r2, mrepS, OP.subtract))
            P.add("v", _tt(z0[m], z0[m], rrepS, OP.mult, sem=sV))
            prev_v = sV.n
            if t_z0 is not None:
                P.add("g", _w(sV, prev_v))
                qG.dma(t_z0[m * 128:(m + 1) * 128, :], z0[m])
        z0_t = prev_v

        if upto != "p1":
            chunk_barrier()
            AR.release(p1cm)
            p2m = AR.mark()

            shz2 = [AR.b(512) for _ in range(2)]
            shz_fr = [0, 0]
            msr = AR.f(S)[0:1, :]
            rsrep = AR.f(S)
            rowsb = AR.b(S)
            for sl in range(4):
                isl = slice(sl * 512, (sl + 1) * 512)
                if sl > 0:
                    P.add("t", _w(sV, sV.n))
                for och in range(DCH):
                    sp = och % 2
                    if shz_fr[sp]:
                        P.add("s", _w(sT, shz_fr[sp]))
                    P.add("s", _act(shz2[sp], z0[och][:, isl],
                                    AF.Square, sem=sS))
                    if och == 0 and sl == 0:
                        banks.twait(4)
                    P.add("t", _w(sS, sS.n))
                    P.add("t", _mm(ps[4][0:1, :], ones_col, shz2[sp],
                                   och == 0, och == DCH - 1, sem=sT))
                    shz_fr[sp] = sT.n
                P.add("v", _w(sT, sT.n))
                P.add("v", _ts(msr[:, isl], ps[4][0:1, :], 1.0 / D, OP.mult,
                               sem=sV))
            banks.set_free(4, sV, sV.n)
            P.add("v", _ts(msr, msr, c_eps[0:1, 1:2], OP.add, sem=sV))
            P.add("s", _w(sV, sV.n))
            P.add("s", _act(msr, msr, AF.Sqrt, sem=sS))
            P.add("v", _w(sS, sS.n))
            P.add("v", _recip(msr, msr))
            P.add("v", _memset(srow[0:1, 9:10], 0.0, sem=sV))
            P.add("s", _w(sV, sV.n))
            P.add("s", _act(rowsb[0:1, :], msr, AF.Copy, sem=sS))
            P.add("t", _w(sS, sS.n))
            replicate_row(rowsb[0:1, :], rsrep, 4)
            P.add("v", _w(sS, sS.n))
            n1t = 0
            for och in range(DCH):
                P.add("v", _stt(n1z[och], z0[och], c_n1w[:, och:och + 1],
                                rsrep, OP.mult, OP.mult,
                                sem=sV if och == DCH - 1 else None))
            n1t = sV.n
            for och in range(DCH):
                qG.dma(z0sp[och * 128:(och + 1) * 128, :], z0[och],
                       extra_wait=(sV, n1t) if och == 0 else None)
            qG.drain()
            chunk_barrier()

            AR.release(zmark)
            c_cos = AR.b(S)
            c_sin = AR.b(S)
            cos_tk = qG.dma(c_cos, cosTd)
            sin_tk = qG.dma(c_sin, sinTd)
            atile = AR.btiles(2, S)
            atile_fr = [None, None]
            at_spill = [None] * 8
            v_aug = AR.btiles(16, 16 * 65)
            qb = AR.btiles(4, S)
            kb = AR.btiles(4, S)
            q2b = [AR.b(512) for _ in range(2)]
            q2fr = [0, 0]
            qnkn = AR.f(S)
            qn = qnkn[0:8, :]
            kn = qnkn[32:40, :]
            mtl = qnkn[64:72, :]
            kmx = AR.f(16)
            mtl_bf = AR.b(S)
            mrep4 = [AR.f(512) for _ in range(4)]
            sprm = [AR.f(512) for _ in range(2)]
            sprm_fr = [0, 0]
            ptil = [AR.b(512) for _ in range(2)]
            ptil_fr = [0, 0]
            brep = AR.f(512)
            brep_fr = [0]
            rbuf = AR.f(512)
            rrw = AR.f(512)
            vwb = AR.b(8 * 512)
            t1b_ = AR.b(512)
            t2b_ = AR.f(512)

            for tt in range(16):
                P.add("v", _memset(_ap3(v_aug[tt][:, 64:], (65, 16), (1, 1)),
                                   1.0, sem=sV if tt == 15 else None))
            vones_t = sV.n

            for H in range(2):
                base_q, base_k, base_v = H * 512, D + H * 512, 2 * D + H * 512
                for src0, dstt, bc0 in ((base_q, qb, H * 4),
                                        (base_k, kb, 8 + H * 4)):
                    for mt in range(4):
                        wsrc = qkvT[:, src0 + mt * 128:src0 + (mt + 1) * 128] \
                            .rearrange("(k p) c -> p k c", p=128)
                        sl_, wbf, tk = wpool.load(wsrc, 1024, inner=128)
                        for n4 in range(4):
                            nsl = slice(n4 * 512, (n4 + 1) * 512)
                            b0 = n4 % 2
                            banks.twait(b0)
                            if n4 == 0:
                                P.add("t", _w(*tk))
                            for k in range(DCH):
                                P.add("t", _mm(ps[b0],
                                               wbf[:, k * 128:(k + 1) * 128],
                                               n1z[k][:, nsl], k == 0,
                                               k == DCH - 1,
                                               sem=sT if k == DCH - 1 else None))
                            tt_ = sT.n
                            if n4 == 3:
                                wpool.consumed(sl_, sT, tt_)
                            P.add("s", _w(sT, tt_))
                            P.add("s", _act(dstt[mt][:, nsl], ps[b0],
                                            AF.Identity,
                                            bias=c_qkvb[:, bc0 + mt:bc0 + mt + 1],
                                            sem=sS))
                            banks.set_free(b0, sS, sS.n)
                t_kpre = tap(f"kpre{H}", [512, S])
                if t_kpre is not None:
                    P.add("g", _w(sS, sS.n))
                    for it in range(4):
                        qG.dma(t_kpre[it * 128:(it + 1) * 128, :], kb[it])
                if H == 0:
                    P.add("v", _w(*cos_tk))
                    P.add("v", _w(*sin_tk))
                for tile in list(qb) + list(kb):
                    for n4 in range(4):
                        nsl = slice(n4 * 512, (n4 + 1) * 512)
                        banks.twait(6)
                        P.add("t", _w(sS, sS.n))
                        P.add("t", _mm(ps[6], c_rotP[:], tile[:, nsl], True, True,
                                       sem=sT))
                        P.add("v", _w(sT, sT.n))
                        P.add("v", _tt(t1b_, tile[:, nsl], c_cos[:, nsl],
                                       OP.mult))
                        P.add("v", _tt(t2b_, ps[6], c_sin[:, nsl], OP.mult))
                        P.add("v", _tt(tile[:, nsl], t1b_, t2b_, OP.add,
                                       sem=sV))
                        banks.set_free(6, sV, sV.n)
                rope_t = sV.n
                vw = qkvT[:, base_v:base_v + 512] \
                    .rearrange("(k p) c -> p k c", p=128)
                vtk = qG.dma(vwb.rearrange("p (k c) -> p k c", c=512), vw,
                             extra_wait=(sT, vmm_t) if H else None)
                bvsl = c_bv[0:1, H * 512:(H + 1) * 512]
                if H:
                    P.add("s", _w(sV, brep_fr[0]))
                P.add("s", _act(rowsb[0:1, 0:512], bvsl, AF.Copy, sem=sS))
                P.add("t", _w(sS, sS.n))
                replicate_row(rowsb[0:1, 0:512], brep, 5)
                brt = sS.n
                for tt in range(16):
                    b0 = 2 + tt % 2
                    banks.twait(b0)
                    if tt == 0:
                        P.add("t", _w(*vtk))
                        P.add("t", _w(sV, vones_t))
                    for k in range(DCH):
                        P.add("t", _mm(ps[b0], n1z[k][:, tt * 128:(tt + 1) * 128],
                                       vwb[:, k * 512:(k + 1) * 512], k == 0,
                                       k == DCH - 1,
                                       sem=sT if k == DCH - 1 else None))
                    P.add("v", _w(sT, sT.n))
                    if tt == 0:
                        P.add("v", _w(sS, brt))
                    vdst = bass.AP(v_aug[tt].tensor,
                                   v_aug[tt].offset + H * 8 * 65,
                                   [v_aug[tt].ap[0], [65, 8], [1, 64]])
                    P.add("v", _tt(vdst, ps[b0], brep, OP.add, sem=sV))
                    banks.set_free(b0, sV, sV.n)
                    if tt == 15:
                        vmm_t = sT.n
                        brep_fr[0] = sV.n
                for srct, dstn in ((qb, qn), (kb, kn)):
                    for qsl in range(4):
                        nsl = slice(qsl * 512, (qsl + 1) * 512)
                        if not (srct is qb and qsl == 0):
                            P.add("t", _w(sV, sV.n))
                        for i in range(4):
                            sp = i % 2
                            if q2fr[sp]:
                                P.add("s", _w(sT, q2fr[sp]))
                            if i == 0 and qsl == 0 and srct is qb:
                                P.add("s", _w(sV, rope_t))
                            P.add("s", _act(q2b[sp], srct[i][:, nsl],
                                            AF.Square, sem=sS))
                            banks.twait(6)
                            P.add("t", _w(sS, sS.n))
                            P.add("t", _mm(ps[6][0:8, :],
                                           c_hbq[:, i * 8:(i + 1) * 8],
                                           q2b[sp], i == 0, i == 3,
                                           sem=sT))
                            q2fr[sp] = sT.n
                        P.add("v", _w(sT, sT.n))
                        P.add("v", _vcopy(dstn[:, nsl], ps[6][0:8, :],
                                          sem=sV))
                        banks.set_free(6, sV, sV.n)
                P.add("v", _red(kmx[0:8, 0:1], kn, OP.max))
                P.add("v", _memset(srow[0:1, 10:11], 0.0, sem=sV))
                P.add("s", _w(sV, sV.n))
                P.add("s", _act(kmx[0:8, 0:1], kmx[0:8, 0:1], AF.Sqrt))
                P.add("s", _act(qn, qn, AF.Sqrt, sem=sS))
                P.add("v", _w(sS, sS.n))
                P.add("v", _ts(mtl, qn, kmx[0:8, 0:1],
                               OP.mult, sem=sV))
                P.add("s", _w(sV, sV.n))
                P.add("s", _act(mtl_bf[0:8, :], mtl, AF.Copy, sem=sS))
                mtl_t = sS.n
                t_q = tap(f"qx{H}", [512, S])
                t_k = tap(f"kx{H}", [512, S])
                if t_q is not None:
                    for it in range(4):
                        qG.dma(t_q[it * 128:(it + 1) * 128, :], qb[it],
                               extra_wait=(sV, rope_t))
                        if t_k is not None:
                            qG.dma(t_k[it * 128:(it + 1) * 128, :], kb[it])
                t_mt = tap(f"mt{H}", [8, S])
                if t_mt is not None:
                    P.add("v", _w(sS, mtl_t))
                    P.add("v", _vcopy(qnkn[96:104, :], mtl, sem=sV))
                    qG.dma(t_mt, qnkn[96:104, :], extra_wait=(sV, sV.n))

                for hh in range(8):
                    hg = H * 8 + hh
                    ti, ro = hh // 2, (hh % 2) * 64
                    for qsl in range(4):
                        banks.twait(6)
                        if hh == 0 and qsl == 0:
                            P.add("t", _w(sS, mtl_t))
                        P.add("t", _mm(ps[6],
                                       c_sel[0:8, hh * 128:(hh + 1) * 128],
                                       mtl_bf[0:8, qsl * 512:(qsl + 1) * 512],
                                       True, True, sem=sT))
                        P.add("s", _w(sT, sT.n))
                        P.add("s", _act(mrep4[qsl], ps[6], AF.Copy, sem=sS))
                        banks.set_free(6, sS, sS.n)
                    mrt = sS.n
                    pend = None

                    def _flush_acc():
                        kt_p, qsl_p, sp_p, exp_t = pend
                        P.add("t", _w(sS, exp_t))
                        P.add("t", _mm(ps[2 + qsl_p][0:65, :],
                                       v_aug[kt_p][:, hg * 65:hg * 65 + 65],
                                       ptil[sp_p], kt_p == 0, kt_p == 15,
                                       sem=sT))
                        ptil_fr[sp_p] = sT.n

                    for kt in range(16):
                        for qsl in range(4):
                            ib = (kt * 4 + qsl)
                            b0 = ib % 2
                            sp = ib % 2
                            banks.twait(b0)
                            P.add("t", _mm(ps[b0],
                                           kb[ti][ro:ro + 64,
                                                  kt * 128:(kt + 1) * 128],
                                           qb[ti][ro:ro + 64,
                                                  qsl * 512:(qsl + 1) * 512],
                                           True, True, sem=sT))
                            sc_t = sT.n
                            if sprm_fr[sp]:
                                P.add("v", _w(sS, sprm_fr[sp]))
                            P.add("v", _w(sT, sc_t))
                            if kt == 0 and qsl == 0:
                                P.add("v", _w(sS, mrt))
                            P.add("v", _tt(sprm[sp], ps[b0], mrep4[qsl],
                                           OP.subtract, sem=sV))
                            banks.set_free(b0, sV, sV.n)
                            if ptil_fr[sp]:
                                P.add("s", _w(sT, ptil_fr[sp]))
                            P.add("s", _w(sV, sV.n))
                            P.add("s", _act(ptil[sp], sprm[sp], AF.Exp,
                                            sem=sS))
                            if pend is not None:
                                _flush_acc()
                            pend = (kt, qsl, sp, sS.n)
                            sprm_fr[sp] = sS.n
                    _flush_acc()
                    pend = None
                    acc_t = sT.n
                    for qsl in range(4):
                        P.add("v", _w(sT, acc_t))
                        P.add("v", _recip(rbuf[0:1, :],
                                          ps[2 + qsl][64:65, :]))
                        P.add("v", _memset(srow[0:1, 11:12], 0.0, sem=sV))
                        P.add("s", _w(sV, sV.n))
                        P.add("s", _act(rowsb[0:1, 0:512], rbuf[0:1, :],
                                        AF.Copy, sem=sS))
                        banks.twait(6)
                        P.add("t", _w(sS, sS.n))
                        P.add("t", _mm(ps[6][0:64, :], c_ones[0:1, 0:64],
                                       rowsb[0:1, 0:512], True, True, sem=sT))
                        P.add("s", _w(sT, sT.n))
                        P.add("s", _act(rrw[0:64, :], ps[6][0:64, :], AF.Copy,
                                        sem=sS))
                        banks.set_free(6, sS, sS.n)
                        P.add("v", _w(sS, sS.n))
                        pslot = (hh // 2) % 2
                        if qsl == 0 and ro == 0 and atile_fr[pslot]:
                            P.add("v", _w(*atile_fr[pslot]))
                        P.add("v", _tt(atile[pslot][ro:ro + 64,
                                                    qsl * 512:(qsl + 1) * 512],
                                       ps[2 + qsl][0:64, :], rrw[0:64, :],
                                       OP.mult, sem=sV))
                        banks.set_free(2 + qsl, sV, sV.n)
                    at_t = sV.n
                    if hh % 2 == 1:
                        pr = hg // 2
                        pslot = (hh // 2) % 2
                        atile_fr[pslot] = qG.dma(
                            atsp[pr * 128:(pr + 1) * 128, :], atile[pslot],
                            extra_wait=(sV, at_t))
                        at_spill[pr] = atile_fr[pslot]
            chunk_barrier()

            AR.release(p1mark)
            z = AR.ftiles(DCH, S)
            zend = AR.mark()
            moemark = zend
            attn = AR.btiles(DCH, S)
            atk = [qG.dma(attn[k], atsp[k * 128:(k + 1) * 128, :],
                          extra_wait=at_spill[k])
                   for k in range(DCH)]
            for tkk in atk:
                P.add("t", _w(*tkk))
            t_atr = tap("atr", [D, S])
            if t_atr is not None:
                for k in range(DCH):
                    qG.dma(t_atr[k * 128:(k + 1) * 128, :], attn[k])
            for och in range(DCH):
                wsrc = outwT[:, och * 128:(och + 1) * 128] \
                    .rearrange("(k p) c -> p k c", p=128)
                sl_, wbf, tk = wpool.load(wsrc, 1024, inner=128)
                ztk = qG.dma(z[och], z0sp[och * 128:(och + 1) * 128, :])
                for n4 in range(4):
                    nsl = slice(n4 * 512, (n4 + 1) * 512)
                    b0 = n4 % 2
                    banks.twait(b0)
                    if n4 == 0:
                        P.add("t", _w(*tk))
                    for k in range(DCH):
                        P.add("t", _mm(ps[b0], wbf[:, k * 128:(k + 1) * 128],
                                       attn[k][:, nsl], k == 0, k == DCH - 1,
                                       sem=sT if k == DCH - 1 else None))
                    tt_ = sT.n
                    if n4 == 3:
                        wpool.consumed(sl_, sT, tt_)
                    P.add("v", _w(sT, tt_))
                    if n4 == 0:
                        P.add("v", _w(*ztk))
                    P.add("v", _stt(z[och][:, nsl], ps[b0],
                                    c_outb[:, och:och + 1], z[och][:, nsl],
                                    OP.add, OP.add, sem=sV))
                    banks.set_free(b0, sV, sV.n)
            z1_t = sV.n
            t_z1 = tap("z1", [D, S])
            if t_z1 is not None:
                for och in range(DCH):
                    qG.dma(t_z1[och * 128:(och + 1) * 128, :], z[och],
                           extra_wait=(sV, z1_t))
            chunk_barrier()

            AR.release(moemark)
            n2 = AR.btiles(DCH, S)
            shz2 = [AR.b(512) for _ in range(2)]
            shz_fr = [0, 0]
            msrT = AR.f(S)
            msr = msrT[0:1, :]
            rowsb = msrT[32:33, :].bitcast(BF)[0:1, 0:S]
            rsrep = AR.f(S)
            for sl in range(4):
                isl = slice(sl * 512, (sl + 1) * 512)
                if sl > 0:
                    P.add("t", _w(sV, sV.n))
                for och in range(DCH):
                    sp = och % 2
                    if shz_fr[sp]:
                        P.add("s", _w(sT, shz_fr[sp]))
                    P.add("s", _act(shz2[sp], z[och][:, isl],
                                    AF.Square, sem=sS))
                    if och == 0 and sl == 0:
                        banks.twait(4)
                    P.add("t", _w(sS, sS.n))
                    P.add("t", _mm(ps[4][0:1, :], ones_col, shz2[sp],
                                   och == 0, och == DCH - 1, sem=sT))
                    shz_fr[sp] = sT.n
                P.add("v", _w(sT, sT.n))
                P.add("v", _ts(msr[:, isl], ps[4][0:1, :], 1.0 / D, OP.mult,
                               sem=sV))
            banks.set_free(4, sV, sV.n)
            P.add("v", _ts(msr, msr, c_eps[0:1, 1:2], OP.add, sem=sV))
            P.add("s", _w(sV, sV.n))
            P.add("s", _act(msr, msr, AF.Sqrt, sem=sS))
            P.add("v", _w(sS, sS.n))
            P.add("v", _recip(msr, msr))
            P.add("v", _memset(srow[0:1, 9:10], 0.0, sem=sV))
            P.add("s", _w(sV, sV.n))
            P.add("s", _act(rowsb[0:1, :], msr, AF.Copy, sem=sS))
            P.add("t", _w(sS, sS.n))
            replicate_row(rowsb[0:1, :], rsrep, 4, bp=32)
            P.add("v", _w(sS, sS.n))
            for och in range(DCH):
                P.add("v", _stt(n2[och], z[och], c_n2w[:, och:och + 1],
                                rsrep, OP.mult, OP.mult,
                                sem=sV if och == DCH - 1 else None))
            n2t = sV.n
            t_n2 = tap("n2", [D, S])
            if t_n2 is not None:
                for och in range(DCH):
                    qG.dma(t_n2[och * 128:(och + 1) * 128, :], n2[och],
                           extra_wait=(sV, n2t))

            lgT = AR.f(128)
            mxa = AR.f(128)
            dsc = AR.f(8)
            e1b = AR.f(8)
            e2b = AR.f(8)
            mskb = AR.f(8)
            ddc = AR.f(16)
            p1c_ = AR.f(16)
            p2c_ = AR.f(16)
            combf = AR.b(128)
            combT = AR.b(S)
            for tt in range(16):
                b0 = 4 + tt % 2
                banks.twait(b0)
                if tt == 0:
                    P.add("t", _w(sV, n2t))
                for k in range(DCH):
                    P.add("t", _mm(ps[b0][:, 0:8],
                                   n2[k][:, tt * 128:(tt + 1) * 128],
                                   c_gw[:, k * 8:(k + 1) * 8],
                                   k == 0, k == DCH - 1,
                                   sem=sT if k == DCH - 1 else None))
                P.add("v", _w(sT, sT.n))
                P.add("v", _vcopy(lgT[:, tt * 8:(tt + 1) * 8], ps[b0][:, 0:8],
                                  sem=sV))
                banks.set_free(b0, sV, sV.n)
            t_lg = tap("lgt", [128, 128])
            if t_lg is not None:
                P.add("v", _ts(srow[0:1, 14:15], srow[0:1, 14:15], 0.0,
                               OP.mult, sem=sV))
                qG.dma(t_lg, lgT, extra_wait=(sV, sV.n))
            for tt in range(16):
                P.add("v", _max8(mxa[:, tt * 8:(tt + 1) * 8],
                                 lgT[:, tt * 8:(tt + 1) * 8]))
            P.add("v", _memset(srow[0:1, 12:13], 0.0, sem=sV))
            gjnk = AR.f(8)
            for tt in range(16):
                lgt = lgT[:, tt * 8:(tt + 1) * 8]
                v1 = mxa[:, tt * 8:tt * 8 + 1]
                v2 = mxa[:, tt * 8 + 1:tt * 8 + 2]
                v1b = bass.AP(v1.tensor, v1.offset, [v1.ap[0], [0, 8]])
                v2b = bass.AP(v2.tensor, v2.offset, [v2.ap[0], [0, 8]])
                P.add("v", _tt(dsc, v1b, lgt, OP.subtract))
                P.add("v", _tt(ddc[:, tt:tt + 1], v1, v2, OP.subtract,
                               sem=sV))
                P.add("s", _w(sV, sV.n))
                P.add("s", _act(p1c_[:, tt:tt + 1], ddc[:, tt:tt + 1],
                                AF.Sigmoid))
                P.add("s", _act(p2c_[:, tt:tt + 1], ddc[:, tt:tt + 1],
                                AF.Sigmoid, scale=-1.0, sem=sS))
                P.add("v", _ts(e1b, dsc, -1.0, OP.mult, -1e-20, OP.is_ge))
                if tt == 0:
                    t_d1 = tap("d1t", [128, 8])
                    t_e1 = tap("e1t", [128, 8])
                    if t_d1 is not None:
                        P.add("v", _ts(srow[0:1, 15:16], srow[0:1, 15:16],
                                       0.0, OP.mult, sem=sV))
                        qG.dma(t_d1, dsc, extra_wait=(sV, sV.n))
                        qG.dma(t_e1, e1b)
                        P.add("v", _w(*qG.dma(tap("v1t", [128, 8]) or t_d1,
                                              mxa[:, 0:8])))
                P.add("v", _ts(gjnk, lgt, 1.0, OP.mult))
                P.add("v", _stt(mskb, e1b, -1e30, lgt, OP.mult, OP.add))
                P.add("v", _ts(gjnk, lgt, 1.0, OP.mult))
                P.add("v", _tt(dsc, v2b, mskb, OP.subtract))
                P.add("v", _w(sS, sS.n))
                P.add("v", _ts(combf[:, tt * 8:(tt + 1) * 8], e1b,
                               p1c_[:, tt:tt + 1], OP.mult))
                P.add("v", _ts(e2b, dsc, -1.0, OP.mult, -1e-20, OP.is_ge))
                P.add("v", _ts(gjnk, lgt, 1.0, OP.mult))
                P.add("v", _stt(combf[:, tt * 8:(tt + 1) * 8], e2b,
                                p2c_[:, tt:tt + 1],
                                combf[:, tt * 8:(tt + 1) * 8], OP.mult,
                                OP.add, sem=sV))
            comb_t = sV.n
            t_mx = tap("mxa", [128, 128])
            if t_mx is not None:
                qG.dma(t_mx, mxa, extra_wait=(sV, comb_t))
            t_dd = tap("ddt", [128, 16])
            if t_dd is not None:
                qG.dma(t_dd, ddc, extra_wait=(sV, comb_t))
                qG.dma(tap("p1t", [128, 16]), p1c_, extra_wait=(sV, comb_t))
                qG.dma(tap("p2t", [128, 16]), p2c_, extra_wait=(sV, comb_t))
            t_cf = tap("combf", [128, 128])
            if t_cf is not None:
                qG.dma(t_cf, combf, extra_wait=(sV, comb_t))
            for rnd in range(2):
                P.add("t", _w(sV, comb_t if rnd == 0 else sV.n))
                for q in range(8):
                    tt = rnd * 8 + q
                    P.add("t", _trans(
                        bass.AP(psbf.tensor, psbf.offset + q * 128,
                                [[psbf.ap[0][0], 8], [1, 128]]),
                        combf[:, tt * 8:(tt + 1) * 8], c_ident[:, 0:128],
                        sem=sT if q == 7 else None))
                P.add("v", _w(sT, sT.n))
                for q in range(8):
                    tt = rnd * 8 + q
                    P.add("v", _vcopy(combT[0:8, tt * 128:(tt + 1) * 128],
                                      psbf[0:8, q * 128:(q + 1) * 128],
                                      sem=sV if q == 7 else None))
            combT_t = sV.n
            t_comb = tap("comb", [8, S])
            if t_comb is not None:
                qG.dma(t_comb, combT[0:8, :], extra_wait=(sV, combT_t))
            chunk_barrier()

            moem2 = AR.mark()
            h_bf = AR.btiles(MCH, S // 2)
            crep = AR.b(S // 2)
            crep_fr = [0]
            sab = [AR.b(512) for _ in range(2)]
            sbb2 = [AR.b(512) for _ in range(2)]
            sab_fr = [0, 0]
            tbb = [AR.f(512) for _ in range(2)]
            tbb_fr = [0, 0]
            for th in range(2):
                ts0 = th * (S // 2)
                for e in range(NE):
                    for q2_ in range(2):
                        banks.twait(6)
                        if e == 0 and q2_ == 0 and th == 0:
                            P.add("t", _w(sV, combT_t))
                        P.add("t", _mm(ps[6],
                                       c_sel[0:8, e * 128:(e + 1) * 128],
                                       combT[0:8, ts0 + q2_ * 512:
                                             ts0 + (q2_ + 1) * 512],
                                       True, True, sem=sT))
                        P.add("s", _w(sT, sT.n))
                        if crep_fr[0]:
                            P.add("s", _w(sV, crep_fr[0]))
                        P.add("s", _act(crep[:, q2_ * 512:(q2_ + 1) * 512],
                                        ps[6], AF.Copy, sem=sS))
                        banks.set_free(6, sS, sS.n)
                    crt = sS.n
                    if th == 0 and e == 0:
                        tba = tap("cba", [8, S])
                        if tba is not None:
                            dtk = qG.dma(tba, combT[0:8, :],
                                         extra_wait=(sS, crt))
                            P.add("s", _w(*dtk))
                    for m in range(MCH):
                        w1src = w1T[e][:, m * 128:(m + 1) * 128] \
                            .rearrange("(k p) c -> p k c", p=128)
                        w2src = w2T[e][:, m * 128:(m + 1) * 128] \
                            .rearrange("(k p) c -> p k c", p=128)
                        sl1, wb1, tk1 = wpool.load(w1src, 1024, inner=128)
                        sl2, wb2, tk2 = wpool.load(w2src, 1024, inner=128)
                        for n2_ in range(2):
                            nsl = slice(ts0 + n2_ * 512, ts0 + (n2_ + 1) * 512)
                            banks.twait(0)
                            banks.twait(1)
                            if n2_ == 0:
                                P.add("t", _w(*tk1))
                                P.add("t", _w(*tk2))
                            for k in range(DCH):
                                P.add("t", _mm(ps[0],
                                               wb1[:, k * 128:(k + 1) * 128],
                                               n2[k][:, nsl], k == 0,
                                               k == DCH - 1))
                                P.add("t", _mm(ps[1],
                                               wb2[:, k * 128:(k + 1) * 128],
                                               n2[k][:, nsl], k == 0,
                                               k == DCH - 1,
                                               sem=sT if k == DCH - 1 else None))
                            tt_ = sT.n
                            if n2_ == 1:
                                wpool.consumed(sl1, sT, tt_)
                                wpool.consumed(sl2, sT, tt_)
                            sp = n2_ % 2
                            bcol = e * 16 + m
                            if sab_fr[sp]:
                                P.add("s", _w(sV, sab_fr[sp]))
                            P.add("s", _w(sT, tt_))
                            P.add("s", _act(sab[sp], ps[0], AF.Silu,
                                            bias=c_b1[:, bcol:bcol + 1]))
                            P.add("s", _act(sbb2[sp], ps[1], AF.Identity,
                                            bias=c_b2[:, bcol:bcol + 1],
                                            sem=sS))
                            st_ = sS.n
                            banks.set_free(0, sS, st_)
                            banks.set_free(1, sS, st_)
                            P.add("v", _w(sS, st_))
                            P.add("v", _tt(h_bf[m][:, n2_ * 512:
                                                   (n2_ + 1) * 512],
                                           sab[sp], sbb2[sp], OP.mult,
                                           sem=sV))
                            sab_fr[sp] = sV.n
                    h_t = sV.n
                    if th == 0 and e == 0:
                        tbb_ = tap("cbb", [8, S])
                        if tbb_ is not None:
                            dtk = qG.dma(tbb_, combT[0:8, :],
                                         extra_wait=(sV, h_t))
                            P.add("s", _w(*dtk))
                            P.add("v", _w(*dtk))
                    for och in range(DCH):
                        w3src = w3T[e][:, och * 128:(och + 1) * 128] \
                            .rearrange("(k p) c -> p k c", p=128)
                        sl3, wb3, tk3 = wpool.load(w3src, 2048, inner=128)
                        for n2_ in range(2):
                            nsl_h = slice(n2_ * 512, (n2_ + 1) * 512)
                            nsl_z = slice(ts0 + n2_ * 512,
                                          ts0 + (n2_ + 1) * 512)
                            b0 = 2 + (n2_ % 2)
                            banks.twait(b0)
                            if n2_ == 0:
                                P.add("t", _w(*tk3))
                                P.add("t", _w(sV, h_t))
                            for k in range(MCH):
                                P.add("t", _mm(ps[b0],
                                               wb3[:, k * 128:(k + 1) * 128],
                                               h_bf[k][:, nsl_h], k == 0,
                                               k == MCH - 1,
                                               sem=sT if k == MCH - 1
                                               else None))
                            tt_ = sT.n
                            if n2_ == 1:
                                wpool.consumed(sl3, sT, tt_)
                            sp = n2_ % 2
                            bcol = e * 8 + och
                            if tbb_fr[sp]:
                                P.add("v", _w(sV, tbb_fr[sp]))
                            P.add("v", _w(sT, tt_))
                            if och == 0 and n2_ == 0:
                                P.add("v", _w(sS, crt))
                            P.add("v", _stt(tbb[sp], ps[b0],
                                            c_b3[:, bcol:bcol + 1],
                                            crep[:, nsl_h], OP.add, OP.mult))
                            P.add("v", _tt(z[och][:, nsl_z], z[och][:, nsl_z],
                                           tbb[sp], OP.add, sem=sV))
                            tbb_fr[sp] = sV.n
                            banks.set_free(b0, sV, sV.n)
                            crep_fr[0] = sV.n
                    if th == 0 and e == 1:
                        tb1 = tap("cb1", [8, S])
                        if tb1 is not None:
                            dtk = qG.dma(tb1, combT[0:8, :],
                                         extra_wait=(sS, crt))
                            P.add("s", _w(*dtk))
                        tc1 = tap("cr1", [128, S // 2])
                        if tc1 is not None:
                            dtk = qG.dma(tc1, crep, extra_wait=(sS, crt))
                            P.add("s", _w(*dtk))
                        te1 = tap("ze1", [D, S])
                        if te1 is not None:
                            ze1t = sV.n
                            for och in range(DCH):
                                qG.dma(te1[och * 128:(och + 1) * 128, :],
                                       z[och], extra_wait=(sV, ze1t))
                        th1 = tap("he1", [HM, S // 2])
                        if th1 is not None:
                            for m in range(MCH):
                                qG.dma(th1[m * 128:(m + 1) * 128, :], h_bf[m],
                                       extra_wait=(sV, h_t))
                    if th == 0 and e == NE - 1:
                        te7 = tap("ze7", [D, S])
                        if te7 is not None:
                            ze7t = sV.n
                            for och in range(DCH):
                                qG.dma(te7[och * 128:(och + 1) * 128, :],
                                       z[och], extra_wait=(sV, ze7t))
                    if th == 0 and e == 0:
                        te0 = tap("ze0", [D, S])
                        if te0 is not None:
                            ze0t = sV.n
                            for och in range(DCH):
                                qG.dma(te0[och * 128:(och + 1) * 128, :],
                                       z[och], extra_wait=(sV, ze0t))
                        th0 = tap("he0", [HM, S // 2])
                        if th0 is not None:
                            for m in range(MCH):
                                qG.dma(th0[m * 128:(m + 1) * 128, :], h_bf[m],
                                       extra_wait=(sV, h_t))
                        tc0 = tap("cr0", [128, S // 2])
                        if tc0 is not None:
                            dtk = qG.dma(tc0, crep, extra_wait=(sS, crt))
                            P.add("s", _w(*dtk))
            z2_t = sV.n
            t_z2 = tap("z2", [D, S])
            if t_z2 is not None:
                for och in range(DCH):
                    qG.dma(t_z2[och * 128:(och + 1) * 128, :], z[och],
                           extra_wait=(sV, z2_t))
            chunk_barrier()

            AR.release(zend)
            shf = [AR.b(512) for _ in range(2)]
            shf2 = [AR.b(512) for _ in range(2)]
            shf_fr = [0, 0]
            m2rep = AR.f(S)
            r2rep = AR.f(S)
            x1rl2 = [AR.f(S) for _ in range(2)]
            x1_fr = [0, 0]
            x2b = [AR.f(S) for _ in range(2)]
            x2_fr = [None, None]
            rows3B = AR.b(S)
            rows3 = [rows3B[0:1, :], rows3B[32:33, :]]
            rowsF2 = AR.f(S)
            mt2 = rowsF2[0:1, 0:S]
            vt2 = rowsF2[32:33, 0:S]
            rt2 = rowsF2[64:65, 0:S]
            q2f = m2rep[32:33, 0:S]
            P.add("v", _memset(sumc, 0.0))
            P.add("v", _memset(sumc2, 0.0, sem=sV))
            zs_t = sV.n
            first = True
            for och in range(DCH):
                for i in range(4):
                    isl = slice(i * 512, (i + 1) * 512)
                    sp = i % 2
                    if shf_fr[sp]:
                        P.add("s", _w(sT, shf_fr[sp]))
                    P.add("s", _act(shf[sp], z[och][:, isl], AF.Copy))
                    P.add("s", _act(shf2[sp], z[och][:, isl], AF.Square,
                                    sem=sS))
                    ct = sS.n
                    if first:
                        banks.twait(4)
                        banks.twait(5)
                        P.add("t", _w(sV, zs_t))
                        first = False
                    else:
                        P.add("t", _w(sV, sV.n))
                    P.add("t", _w(sS, ct))
                    P.add("t", _mm(ps[4][0:1, :], ones_col, shf[sp],
                                   True, True))
                    P.add("t", _mm(ps[5][0:1, :], ones_col, shf2[sp],
                                   True, True, sem=sT))
                    shf_fr[sp] = sT.n
                    P.add("v", _w(sT, sT.n))
                    P.add("v", _tt(sumc[:, isl], sumc[:, isl], ps[4][0:1, :],
                                   OP.add))
                    P.add("v", _tt(sumc2[:, isl], sumc2[:, isl],
                                   ps[5][0:1, :], OP.add, sem=sV))
            banks.set_free(4, sV, sV.n)
            banks.set_free(5, sV, sV.n)
            P.add("v", _ts(mt2, sumc, 1.0 / D, OP.mult))
            P.add("v", _ts(vt2, sumc2, 1.0 / D, OP.mult))
            P.add("v", _tt(q2f, mt2, mt2, OP.mult))
            P.add("v", _tt(vt2, vt2, q2f, OP.subtract, sem=sV))
            P.add("s", _w(sV, sV.n))
            P.add("s", _act(vt2, vt2, AF.Sqrt, bias=c_eps[32:33, 0:1], sem=sS))
            P.add("v", _w(sS, sS.n))
            P.add("v", _recip(rt2, vt2))
            P.add("v", _memset(srow[0:1, 13:14], 0.0, sem=sV))
            P.add("s", _w(sV, sV.n))
            P.add("s", _act(rows3[0], mt2, AF.Copy))
            P.add("s", _act(rows3[1], rt2, AF.Copy, sem=sS))
            P.add("t", _w(sS, sS.n))
            replicate_row(rows3[0], m2rep, 4)
            replicate_row(rows3[1], r2rep, 5, bp=32)
            frep_t = sS.n
            P.add("v", _w(sS, frep_t))
            for och in range(DCH):
                sp = och % 2
                ew = (sV, x1_fr[sp]) if x1_fr[sp] else None
                xtk = qG.dma(x1rl2[sp], x1sp[och * 128:(och + 1) * 128, :],
                             extra_wait=ew)
                P.add("v", _w(*xtk))
                if x2_fr[sp] is not None:
                    P.add("v", _w(*x2_fr[sp]))
                P.add("v", _tt(x2b[sp], z[och], m2rep, OP.subtract))
                P.add("v", _tt(x2b[sp], x2b[sp], r2rep, OP.mult))
                P.add("v", _tt(x2b[sp], x1rl2[sp], x2b[sp], OP.add, sem=sV))
                x1_fr[sp] = sV.n
                x2_fr[sp] = qG.dma(x2T[och * 128:(och + 1) * 128, :],
                                   x2b[sp], extra_wait=(sV, sV.n))

        chunk_barrier()
        qG.drain()
        qW.drain()

        with nc.Block() as block:
            @block.sync
            def _(eng):
                for f in P.ops["sy"]:
                    f(eng)

            @block.gpsimd
            def _(eng):
                for f in P.ops["g"]:
                    f(eng)

            @block.scalar
            def _(eng):
                for f in P.ops["s"]:
                    f(eng)

            @block.vector
            def _(eng):
                for f in P.ops["v"]:
                    f(eng)

            @block.tensor
            def _(eng):
                for f in P.ops["t"]:
                    f(eng)

    return nc


def emit_phase2(env):
    raise NotImplementedError


_CACHE = {}


def _prep_shared(i):
    asc = np.ascontiguousarray
    sh = {}
    sh["w_inT"] = asc(i["L0_attn_in_proj_w"].T)
    cw = i["L0_attn_conv_w"].reshape(DI, 4)
    sh["convw"] = asc(cw.reshape(16, 128, 4).transpose(1, 0, 2).reshape(128, 64))
    sh["convb"] = asc(i["L0_attn_conv_b"].reshape(16, 128).T)
    xp = np.zeros((DI, 128), np.float32)
    xp[:, 0:64] = i["L0_attn_x_proj_w"].T[:, 0:64]
    xp[:, 64:80] = i["L0_attn_x_proj_w"].T[:, 64:80]
    xp[:, 96:112] = i["L0_attn_x_proj_w"].T[:, 80:96]
    sh["xprojT"] = xp
    sh["dtprojT"] = asc(i["L0_attn_dt_proj_w"].T)
    sh["dtbd"] = asc(i["L0_attn_dt_proj_b"].reshape(16, 128).T)
    sh["alogd"] = asc(i["L0_attn_A_log"].reshape(16, 128, 16)
                      .transpose(1, 0, 2).reshape(128, 256))
    sh["ddev"] = asc(i["L0_attn_D"].reshape(16, 128).T)
    sh["outpT"] = asc(i["L0_attn_out_proj_w"].T)
    sh["f1T"] = asc(i["L0_attn_ffn_w1"].T)
    sh["fb1d"] = asc(i["L0_attn_ffn_b1"].reshape(32, 128).T)
    sh["f2T"] = asc(i["L0_attn_ffn_w2"].T)
    sh["fb2d"] = asc(i["L0_attn_ffn_b2"].reshape(8, 128).T)
    sh["n1wd"] = asc(i["L0_ffn_norm1_w"].reshape(8, 128).T)
    sh["n2wd"] = asc(i["L0_ffn_norm2_w"].reshape(8, 128).T)
    sh["qkvT"] = asc(i["L0_ffn_qkv_w"].T)
    sh["qkvbd"] = asc(i["L0_ffn_qkv_b"].reshape(24, 128).T)
    sh["bvrow"] = asc(i["L0_ffn_qkv_b"][2048:3072].reshape(1, D))
    sh["outwT"] = asc(i["L0_ffn_out_w"].T)
    sh["outbd"] = asc(i["L0_ffn_out_b"].reshape(8, 128).T)
    sh["gwT"] = asc(i["L0_ffn_gate_w"].T)
    sh["w1T"] = asc(i["L0_ffn_w1"].transpose(0, 2, 1))
    sh["w2T"] = asc(i["L0_ffn_w2"].transpose(0, 2, 1))
    sh["w3T"] = asc(i["L0_ffn_w3"].transpose(0, 2, 1))
    sh["b1d"] = asc(i["L0_ffn_b1"].reshape(NE, 16, 128)
                    .transpose(2, 0, 1).reshape(128, 128))
    sh["b2d"] = asc(i["L0_ffn_b2"].reshape(NE, 16, 128)
                    .transpose(2, 0, 1).reshape(128, 128))
    sh["b3d"] = asc(i["L0_ffn_b3"].reshape(NE, 8, 128)
                    .transpose(2, 0, 1).reshape(128, 64))
    sh["identb"] = np.eye(128, dtype=B16)
    sh["onesb"] = np.ones((128, 128), dtype=B16)
    rp = np.zeros((128, 128), np.float32)
    for r in range(128):
        cc = r % 64
        base = r - cc
        if cc < 32:
            rp[base + cc + 32, r] = -1.0
        else:
            rp[base + cc - 32, r] = 1.0
    sh["rotP"] = rp.astype(B16)
    invf = 1.0 / (10000.0 ** (np.arange(0, 64, 2, dtype=np.float64) / 64.0))
    t = np.arange(S, dtype=np.float64)
    fr = t[:, None] * invf[None, :]
    emb = np.concatenate([fr, fr], axis=1)
    sc = 8.0 ** -0.5
    sh["cosT"] = np.tile((np.cos(emb) * sc).T, (2, 1)).astype(np.float32).astype(B16)
    sh["sinT"] = np.tile((np.sin(emb) * sc).T, (2, 1)).astype(np.float32).astype(B16)
    sel = np.zeros((16, 16, 128), np.float32)
    for n in range(16):
        sel[n, n, :] = 1.0
    sh["selb"] = sel.transpose(1, 0, 2).reshape(16, 2048).astype(B16)
    hbq = np.zeros((128, 4, 8), np.float32)
    for i in range(4):
        for p in range(128):
            hbq[p, i, 2 * i + p // 64] = 1.0
    sh["hbq"] = hbq.reshape(128, 32).astype(B16)
    return sh


def _run(inputs, upto=None, taps=None, trace=False):
    upto = upto if upto is not None else _OPTS["upto"]
    taps = tuple(taps if taps is not None else _OPTS["taps"])
    key = (upto, taps)
    if key not in _CACHE:
        _CACHE[key] = build(upto=upto, taps=taps)
    nc = _CACHE[key]
    sh = _prep_shared({k: np.asarray(v) for k, v in inputs.items()})
    x = np.asarray(inputs["L0_attn_seq_embeddings"])
    in_maps = []
    for c in range(8):
        m = dict(sh)
        m["x0T"] = np.ascontiguousarray(x[c].T)
        in_maps.append(m)
    return run_bass_kernel_spmd(nc, in_maps, core_ids=list(range(8)),
                                trace=trace)


def kernel(**inputs):
    res = _run(inputs)
    out = np.stack([np.asarray(r["x2T"]).T for r in res.results], axis=0)
    return out.astype(np.float32)


# version 03 (final)
# speedup vs baseline: 1.0256x; 1.0256x over previous
import numpy as np
import ml_dtypes
import concourse.bass as bass
import concourse.mybir as mybir
from concourse.bass_utils import run_bass_kernel_spmd
from concourse.alu_op_type import AluOpType as OP
from contextlib import ExitStack

F32 = mybir.dt.float32
BF = mybir.dt.bfloat16
AF = mybir.ActivationFunctionType
AX = mybir.AxisListType

S, D, DI, DS, RK, HF, NE, HM, NH, HD = 2048, 1024, 2048, 16, 64, 4096, 8, 2048, 16, 64
TC = 512
NCH = S // TC
DCH = D // 128
ICH = DI // 128
FCH = HF // 128
MCH = HM // 128
B16 = ml_dtypes.bfloat16

_OPTS = {"upto": "full", "taps": ()}

NENG = ("sy", "g", "s", "v", "t")


class Sem:
    def __init__(self, ctx, nc, name):
        self.h = ctx.enter_context(nc.semaphore(name))
        self.n = 0

    def tick(self, k=1):
        self.n += k
        return self.n


class Prog:
    def __init__(self):
        self.ops = {k: [] for k in NENG}

    def add(self, key, fn):
        self.ops[key].append(fn)


class Arena:

    def __init__(self, ctx, nc, name, total_bytes):
        assert total_bytes % 4 == 0
        self.t = ctx.enter_context(
            nc.sbuf_tensor(name, [128, total_bytes // 4], F32))
        self.off = 0
        self.total = total_bytes
        self.peak = 0
        self.name = name

    def _alloc(self, nbytes, align=64):
        o = (self.off + align - 1) // align * align
        self.off = o + nbytes
        self.peak = max(self.peak, self.off)
        assert self.off <= self.total, \
            f"arena {self.name}: {self.off} > {self.total}"
        return o

    def f(self, cols, align=64):
        o = self._alloc(cols * 4, align)
        return self.t[:, o // 4:o // 4 + cols]

    def b(self, cols, align=64):
        o = self._alloc(cols * 2, align)
        return self.t[:, o // 4:o // 4 + (cols + 1) // 2].bitcast(BF)[:, 0:cols]

    def ftiles(self, n, cols):
        return [self.f(cols) for _ in range(n)]

    def btiles(self, n, cols):
        return [self.b(cols) for _ in range(n)]

    def mark(self):
        return self.off

    def release(self, m):
        self.off = m


def _w(sem, val):
    return lambda e: e.wait_ge(sem.h, val)


def _inc(instr, sem, k):
    if sem is not None:
        instr.then_inc(sem.h, k)


def _tick(sem, k):
    if sem is not None:
        sem.tick(k)


def _dma(dst, src, sem):
    _tick(sem, 16)

    def f(e):
        _inc(e.dma_start(out=dst, in_=src), sem, 16)
    return f


def _mm(out, lhsT, rhs, start, stop, sem=None, k=1):
    _tick(sem, k)

    def f(e):
        _inc(e.matmul(out, lhsT, rhs, start=start, stop=stop,
                      skip_group_check=True), sem, k)
    return f


def _trans(out, in_, ident, sem=None, k=1):
    _tick(sem, k)

    def f(e):
        _inc(e.matmul(out, in_, ident, start=True, stop=True,
                      is_transpose=True, skip_group_check=True), sem, k)
    return f


def _act(out, in_, func, bias=None, scale=None, accum=None, sem=None, k=1):
    _tick(sem, k)
    kw = {}
    if bias is not None:
        kw["bias"] = bias
    if scale is not None:
        kw["scale"] = scale
    if accum is not None:
        kw["accum_out"] = accum

    def f(e):
        _inc(e.activation(out=out, in_=in_, func=func, **kw), sem, k)
    return f


def _tt(out, a, b, op, sem=None, k=1):
    _tick(sem, k)

    def f(e):
        _inc(e.tensor_tensor(out=out, in0=a, in1=b, op=op), sem, k)
    return f


def _stt(out, in0, scalar, in1, op0, op1, sem=None, k=1):
    _tick(sem, k)

    def f(e):
        _inc(e.scalar_tensor_tensor(out=out, in0=in0, scalar=scalar, in1=in1,
                                    op0=op0, op1=op1), sem, k)
    return f


def _ts(out, in0, s1, op0, s2=None, op1=None, sem=None, k=1):
    _tick(sem, k)

    def f(e):
        if op1 is None:
            _inc(e.tensor_scalar(out=out, in0=in0, scalar1=s1, scalar2=None,
                                 op0=op0), sem, k)
        else:
            _inc(e.tensor_scalar(out=out, in0=in0, scalar1=s1, scalar2=s2,
                                 op0=op0, op1=op1), sem, k)
    return f


def _scan(out, d0, d1, init, sem=None, k=1):
    _tick(sem, k)

    def f(e):
        _inc(e.tensor_tensor_scan(out=out, data0=d0, data1=d1, initial=init,
                                  op0=OP.mult, op1=OP.add), sem, k)
    return f


def _vcopy(out, in_, sem=None, k=1):
    _tick(sem, k)

    def f(e):
        _inc(e.tensor_copy(out, in_), sem, k)
    return f


def _memset(ap, v, sem=None, k=1):
    _tick(sem, k)

    def f(e):
        _inc(e.memset(ap, v), sem, k)
    return f


def _red(out, in_, op, sem=None, k=1):
    _tick(sem, k)

    def f(e):
        _inc(e.tensor_reduce(out=out, in_=in_, axis=AX.X, op=op), sem, k)
    return f


def _max8(out, in_, sem=None, k=1):
    _tick(sem, k)

    def f(e):
        _inc(e.max(out=out, in_=in_), sem, k)
    return f


def _recip(out, in_, sem=None, k=1):
    _tick(sem, k)

    def f(e):
        _inc(e.reciprocal(out=out, in_=in_), sem, k)
    return f


def _ap3(src, d0, d1):
    return bass.AP(src.tensor, src.offset, [src.ap[0], list(d0), list(d1)])


class DmaQ:

    def __init__(self, ctx, nc, P, qkey, name, k=4):
        self.P, self.qkey = P, qkey
        self.sems = [Sem(ctx, nc, f"{name}{i}") for i in range(k)]
        self.k = k
        self.i = 0

    def dma(self, dst, src, extra_wait=None):
        sm = self.sems[self.i % self.k]
        self.i += 1
        if extra_wait is not None:
            self.P.add(self.qkey, _w(*extra_wait))
        if sm.n:
            self.P.add(self.qkey, _w(sm, sm.n))
        self.P.add(self.qkey, _dma(dst, src, sm))
        return (sm, sm.n)

    def drain(self):
        for sm in self.sems:
            if sm.n:
                self.P.add(self.qkey, _w(sm, sm.n))


class WPool:

    def __init__(self, q, ar, cols, depth):
        self.q = q
        self.bf = [ar.b(cols) for _ in range(depth)]
        self.depth = depth
        self.bf_free = [None] * depth
        self.i = 0

    def load(self, src_ap, ncols, inner=None):
        sl = self.i % self.depth
        self.i += 1
        bf = self.bf[sl][:, 0:ncols]
        dst = bf if inner is None else bf.rearrange("p (k c) -> p k c", c=inner)
        tk = self.q.dma(dst, src_ap, extra_wait=self.bf_free[sl])
        return sl, bf, tk

    def consumed(self, sl, sem, val):
        self.bf_free[sl] = (sem, val)


class Banks:
    def __init__(self, P):
        self.P = P
        self.free = {}

    def twait(self, b):
        if self.free.get(b):
            self.P.add("t", _w(*self.free[b]))

    def set_free(self, b, sem, val):
        self.free[b] = (sem, val)


def build(upto="full", taps=()):
    nc = bass.Bass("TRN2", target_bir_lowering=False,
                   detect_race_conditions=False)

    def din(name, shape, dtype=F32):
        return nc.dram_tensor(name, shape, dtype, kind="ExternalInput").ap()

    def dout(name, shape, dtype=F32):
        return nc.dram_tensor(name, shape, dtype, kind="ExternalOutput").ap()

    def dint(name, shape, dtype=F32):
        kind = "ExternalOutput" if name in taps else "Internal"
        return nc.dram_tensor(name, shape, dtype, kind=kind).ap()

    x0T = din("x0T", [D, S])
    w_inT = din("w_inT", [D, 2 * DI])
    convw = din("convw", [128, 64])
    convb = din("convb", [128, 16])
    xprojT = din("xprojT", [DI, 128])
    dtprojT = din("dtprojT", [RK, DI])
    dtbd = din("dtbd", [128, 16])
    alogd = din("alogd", [128, 256])
    ddev = din("ddev", [128, 16])
    outpT = din("outpT", [DI, D])
    f1T = din("f1T", [D, HF])
    fb1d = din("fb1d", [128, 32])
    f2T = din("f2T", [HF, D])
    fb2d = din("fb2d", [128, 8])
    n1wd = din("n1wd", [128, 8])
    n2wd = din("n2wd", [128, 8])
    qkvT = din("qkvT", [D, 3 * D])
    qkvbd = din("qkvbd", [128, 24])
    bvrow = din("bvrow", [1, D])
    outwT = din("outwT", [D, D])
    outbd = din("outbd", [128, 8])
    gwT = din("gwT", [D, NE])
    w1T = din("w1T", [NE, D, HM])
    w2T = din("w2T", [NE, D, HM])
    w3T = din("w3T", [NE, HM, D])
    b1d = din("b1d", [128, 128])
    b2d = din("b2d", [128, 128])
    b3d = din("b3d", [128, 64])
    identb = din("identb", [128, 128], BF)
    onesb = din("onesb", [128, 128], BF)
    rotPd = din("rotP", [128, 128], BF)
    cosTd = din("cosT", [128, S], BF)
    sinTd = din("sinT", [128, S], BF)
    selbd = din("selb", [16, 2048], BF)
    hbqd = din("hbq", [128, 32], BF)

    x1sp = dint("x1sp", [D, S])
    z0sp = dint("z0sp", [D, S])
    c_sp = dint("c_sp", [D, S], BF)
    atsp = dint("atsp", [D, S], BF)
    z1sp = dint("z1sp", [D, S])
    x2T = dout("x2T", [D, S])

    tapw = {}

    def tap(name, shape, dtype=F32):
        if name in taps:
            tapw[name] = nc.dram_tensor("dbg_" + name, shape, dtype,
                                        kind="ExternalOutput").ap()
            return tapw[name]
        return None

    P = Prog()
    with ExitStack() as ctx:
        sT = Sem(ctx, nc, "sT")
        sV = Sem(ctx, nc, "sV")
        sS = Sem(ctx, nc, "sS")
        qW = DmaQ(ctx, nc, P, "sy", "qW", 4)
        qG = DmaQ(ctx, nc, P, "g", "qG", 4)
        sGB = Sem(ctx, nc, "sGB")

        AR = Arena(ctx, nc, "arena", 178 * 1024)

        ps_h = [ctx.enter_context(nc.psum_tensor(f"ps{i}", [128, 512], F32))
                for i in range(7)]
        ps = [h[:] for h in ps_h]
        psbf = ctx.enter_context(nc.psum_tensor("psbf", [128, 1024], BF))[:]
        banks = Banks(P)

        def psb(name, shape, dtype):
            return ctx.enter_context(nc.sbuf_tensor(name, shape, dtype))

        c_ident = psb("c_ident", [128, 128], BF)
        c_ones = psb("c_ones", [128, 128], BF)
        c_rotP = psb("c_rotP", [128, 128], BF)
        c_convw = psb("c_convw", [128, 64], F32)
        c_convb = psb("c_convb", [128, 16], F32)
        c_dtb = psb("c_dtb", [128, 16], F32)
        c_A = psb("c_A", [128, 256], F32)
        c_D = psb("c_D", [128, 16], F32)
        c_fb1 = psb("c_fb1", [128, 32], F32)
        c_fb2 = psb("c_fb2", [128, 8], F32)
        c_n1w = psb("c_n1w", [128, 8], F32)
        c_n2w = psb("c_n2w", [128, 8], F32)
        c_qkvb = psb("c_qkvb", [128, 24], F32)
        c_bv = psb("c_bv", [1, D], F32)
        c_outb = psb("c_outb", [128, 8], F32)
        c_b1 = psb("c_b1", [128, 128], F32)
        c_b2 = psb("c_b2", [128, 128], F32)
        c_b3 = psb("c_b3", [128, 64], F32)
        c_xpj = psb("c_xpj", [128, 16 * 128], BF)
        c_dtp = psb("c_dtp", [RK, DI], BF)
        c_gw = psb("c_gw", [128, 64], BF)
        c_sel = psb("c_sel", [16, 2048], BF)
        c_hbq = psb("c_hbq", [128, 32], BF)
        hcar = psb("hcar", [128, 256], F32)
        sums = psb("sums", [128, S], F32)
        srow = psb("srow", [1, 32], F32)
        xm_tail = psb("xm_tail", [128, 48], BF)
        c_eps = psb("c_eps", [128, 2], F32)

        sumc = sums[0:1, :]
        sumc2 = sums[32:33, :]
        sumx = sums[64:65, :]
        sumx2 = sums[96:97, :]
        ones_col = c_ones[:, 0:1]
        ones_row = c_ones[0:1, :]

        for dst, src in [(c_ident, identb), (c_ones, onesb), (c_rotP, rotPd),
                         (c_sel, selbd), (c_hbq, hbqd),
                         (c_convw, convw), (c_convb, convb), (c_dtb, dtbd),
                         (c_D, ddev), (c_fb1, fb1d), (c_fb2, fb2d),
                         (c_n1w, n1wd), (c_n2w, n2wd), (c_qkvb, qkvbd),
                         (c_bv, bvrow), (c_outb, outbd), (c_b1, b1d),
                         (c_b2, b2d), (c_b3, b3d)]:
            qW.dma(dst[:], src)
        aw_tk = qW.dma(c_A[:], alogd)
        P.add("s", _w(*aw_tk))
        P.add("s", _act(c_A[:], c_A[:], AF.Exp, sem=sS))
        P.add("v", _w(sS, sS.n))
        P.add("v", _ts(c_A[:], c_A[:], -1.0, OP.mult, sem=sV))
        a_ready = sV.n
        m0 = AR.mark()
        tmpst = AR.f(2048)
        for dst, src, rows, cols, inner in [
            (c_xpj, xprojT.rearrange("(j p) c -> p j c", p=128), 128, 16 * 128, 128),
            (c_dtp, dtprojT, RK, DI, None),
            (c_gw, gwT.rearrange("(j p) c -> p j c", p=128), 128, 64, 8),
        ]:
            dd = tmpst[0:rows, 0:cols]
            if inner is not None:
                dd = dd.rearrange("p (j c) -> p j c", c=inner)
            tk = qW.dma(dd, src)
            P.add("s", _w(*tk))
            P.add("s", _act(dst[0:rows, 0:cols], tmpst[0:rows, 0:cols],
                            AF.Copy, sem=sS))
            P.add("sy", _w(sS, sS.n))
        AR.release(m0)
        P.add("v", _memset(hcar[:], 0.0))
        P.add("v", _memset(c_eps[:, 0:1], 1e-5))
        P.add("v", _memset(c_eps[:, 1:2], 1e-6))
        P.add("v", _memset(sums[:], 0.0, sem=sV))
        init_v = sV.n

        def chunk_barrier():
            tv, ts_, tt = sV.n, sS.n, sT.n
            qG.drain()
            gb = sGB.tick()
            P.add("g", _w(sV, tv))
            P.add("g", _w(sS, ts_))
            P.add("g", _w(sT, tt))
            P.add("g", lambda e, h=sGB.h: e.engine_nop().then_inc(h, 1))
            P.add("v", _w(sS, ts_))
            P.add("v", _w(sT, tt))
            P.add("v", _w(sGB, gb))
            P.add("s", _w(sV, tv))
            P.add("s", _w(sT, tt))
            P.add("s", _w(sGB, gb))
            P.add("t", _w(sV, tv))
            P.add("t", _w(sS, ts_))
            P.add("t", _w(sGB, gb))

        def replicate_row(src_row_bf, dst_f32, bank, bp=0):
            n = src_row_bf.ap[-1][1]
            nsl = (n + 511) // 512
            orow = c_ones[bp:bp + 1, :]
            for i in range(nsl):
                c0, c1 = i * 512, min(n, (i + 1) * 512)
                banks.twait(bank)
                P.add("t", _mm(ps[bank][0:128, 0:c1 - c0], orow,
                               src_row_bf[:, c0:c1], True, True, sem=sT))
                P.add("s", _w(sT, sT.n))
                P.add("s", _act(dst_f32[:, c0:c1], ps[bank][0:128, 0:c1 - c0],
                                AF.Copy, sem=sS))
                banks.set_free(bank, sS, sS.n)

        wpool = WPool(qG, AR, 2048, 4)
        p1mark = AR.mark()

        n0 = AR.btiles(DCH, TC)
        res_s = AR.btiles(ICH, TC)
        u_bf = AR.btiles(ICH, TC)
        delta = AR.btiles(ICH, TC)
        dt_bf = AR.b(TC)
        B_bf = AR.b(TC)
        C_bf = AR.b(TC)
        B_bc = [AR.b(8 * TC) for _ in range(2)]
        C_bc = [AR.b(8 * TC) for _ in range(2)]
        y2 = AR.btiles(ICH, TC)

        t_u = tap("u", [DI, S])
        t_delta = tap("delta", [DI, S])
        t_y2 = tap("y2", [DI, S])
        t_m = tap("m", [D, S])
        t_n0 = tap("n0", [D, S])

        P.add("t", _w(sV, init_v))
        P.add("t", _w(sV, a_ready))

        bank_rr = {"i": 0}

        def next_pair():
            p = bank_rr["i"] % 3
            bank_rr["i"] += 1
            return (2 * p, 2 * p + 1)

        regmark = AR.mark()

        for c in range(NCH):
            csl = slice(c * TC, (c + 1) * TC)
            chunk_barrier()
            AR.release(regmark)
            x0c = AR.f(DCH * TC)
            x0cT = [x0c[:, m * TC:(m + 1) * TC] for m in range(DCH)]
            shad8 = AR.btiles(DCH, TC)
            shsq8 = AR.btiles(DCH, TC)
            rowb0 = AR.b(TC)
            rowb1 = AR.b(TC)
            mu_row = AR.f(TC)[0:1, :]
            r_row = AR.f(TC)[0:1, :]
            v_scr = AR.f(TC)[0:1, :]
            murep = AR.f(TC)
            rrep = AR.f(TC)

            gx_tk = qG.dma(x0c.rearrange("p (m t) -> p m t", m=DCH),
                           x0T[:, csl].rearrange("(m p) t -> p m t", p=128))
            P.add("s", _w(*gx_tk))
            for m in range(DCH):
                P.add("s", _act(shad8[m], x0cT[m], AF.Copy))
                P.add("s", _act(shsq8[m], x0cT[m], AF.Square,
                                sem=sS if m == DCH - 1 else None))
            ct = sS.n
            banks.twait(4)
            banks.twait(5)
            P.add("t", _w(sS, ct))
            for m in range(DCH):
                P.add("t", _mm(ps[4][0:1, :], ones_col, shad8[m],
                               m == 0, m == DCH - 1))
                P.add("t", _mm(ps[5][0:1, :], ones_col, shsq8[m],
                               m == 0, m == DCH - 1,
                               sem=sT if m == DCH - 1 else None))
            P.add("v", _w(sT, sT.n))
            P.add("v", _ts(mu_row, ps[4][0:1, :], 1.0 / D, OP.mult))
            P.add("v", _ts(r_row, ps[5][0:1, :], 1.0 / D, OP.mult, sem=sV))
            banks.set_free(4, sV, sV.n)
            banks.set_free(5, sV, sV.n)
            P.add("v", _tt(v_scr, mu_row, mu_row, OP.mult))
            P.add("v", _tt(r_row, r_row, v_scr, OP.subtract, sem=sV))
            P.add("s", _w(sV, sV.n))
            P.add("s", _act(r_row, r_row, AF.Sqrt, bias=c_eps[0:1, 0:1], sem=sS))
            P.add("v", _w(sS, sS.n))
            P.add("v", _recip(r_row, r_row))
            P.add("v", _tt(v_scr, mu_row, mu_row, OP.mult, sem=sV))
            P.add("s", _w(sV, sV.n))
            P.add("s", _act(rowb0[0:1, :], mu_row, AF.Copy))
            P.add("s", _act(rowb1[0:1, :], r_row, AF.Copy, sem=sS))
            P.add("t", _w(sS, sS.n))
            replicate_row(rowb0[0:1, :], murep, 4)
            replicate_row(rowb1[0:1, :], rrep, 5)
            P.add("v", _w(sS, sS.n))
            P.add("v", _w(*gx_tk))
            for m in range(DCH):
                P.add("v", _tt(x0cT[m], x0cT[m], murep, OP.subtract))
                P.add("v", _tt(n0[m], x0cT[m], rrep, OP.mult,
                               sem=sV if m == DCH - 1 else None))
            n0t = sV.n
            if t_n0 is not None:
                P.add("g", _w(sV, n0t))
                for m in range(DCH):
                    qG.dma(t_n0[m * 128:(m + 1) * 128, csl], n0[m])

            AR.release(regmark)
            xm_pad = AR.btiles(ICH, TC + 3)
            cvA = [AR.f(TC) for _ in range(2)]
            cvV = [AR.f(TC) for _ in range(2)]

            if c == 0:
                for j in range(ICH):
                    P.add("v", _memset(xm_pad[j][:, 0:3], 0.0,
                                       sem=sV if j == ICH - 1 else None))
            else:
                for j in range(ICH):
                    P.add("v", _vcopy(xm_pad[j][:, 0:3],
                                      xm_tail[:, j * 3:(j + 1) * 3],
                                      sem=sV if j == ICH - 1 else None))
            tail_t = sV.n

            first_evac = True
            for mg in range(16):
                src = w_inT[:, mg * 256:(mg + 1) * 256] \
                    .rearrange("(k p) c -> p k c", p=128)
                sl, wbf, (csem, cval) = wpool.load(src, 2048, inner=256)
                b0, b1_ = next_pair()
                banks.twait(b0)
                banks.twait(b1_)
                P.add("t", _w(csem, cval))
                P.add("t", _w(sV, n0t))
                P.add("t", _w(sV, tail_t))
                for k in range(DCH):
                    P.add("t", _mm(ps[b0], wbf[:, k * 256:k * 256 + 128],
                                   n0[k], k == 0, k == DCH - 1))
                    P.add("t", _mm(ps[b1_], wbf[:, k * 256 + 128:k * 256 + 256],
                                   n0[k], k == 0, k == DCH - 1,
                                   sem=sT if k == DCH - 1 else None))
                tt = sT.n
                wpool.consumed(sl, sT, tt)
                P.add("s", _w(sT, tt))
                if first_evac:
                    P.add("s", _w(sV, tail_t))
                    first_evac = False
                for pi, mi in ((b0, 2 * mg), (b1_, 2 * mg + 1)):
                    if mi < ICH:
                        P.add("s", _act(xm_pad[mi][:, 3:3 + TC], ps[pi],
                                        AF.Copy, sem=sS))
                    else:
                        P.add("s", _act(res_s[mi - ICH], ps[pi], AF.Silu,
                                        sem=sS))
                banks.set_free(b0, sS, sS.n)
                banks.set_free(b1_, sS, sS.n)
            inproj_t = sS.n

            for j in range(ICH):
                w4 = [c_convw[:, j * 4 + t:j * 4 + t + 1] for t in range(4)]
                if j == 0:
                    P.add("s", _w(sS, inproj_t))
                P.add("s", _act(cvA[0], xm_pad[j][:, 0:TC], AF.Copy,
                                scale=w4[0]))
                P.add("s", _act(cvA[1], xm_pad[j][:, 1:1 + TC], AF.Copy,
                                scale=w4[1], sem=sS))
                at = sS.n
                P.add("v", _w(sS, at))
                P.add("v", _stt(cvV[0], xm_pad[j][:, 2:2 + TC], w4[2],
                                cvA[0], OP.mult, OP.add))
                P.add("v", _stt(cvV[1], xm_pad[j][:, 3:3 + TC], w4[3],
                                cvA[1], OP.mult, OP.add))
                P.add("v", _vcopy(xm_tail[:, j * 3:(j + 1) * 3],
                                  xm_pad[j][:, 512:515]))
                P.add("v", _tt(cvV[0], cvV[0], cvV[1], OP.add, sem=sV))
                vt = sV.n
                P.add("s", _w(sV, vt))
                P.add("s", _act(u_bf[j], cvV[0], AF.Silu,
                                bias=c_convb[:, j:j + 1], sem=sS))
            u_t = sS.n
            if t_u is not None:
                P.add("g", _w(sS, u_t))
                for j in range(ICH):
                    qG.dma(t_u[j * 128:(j + 1) * 128, csl], u_bf[j])

            AR.release(regmark)
            du_b = AR.b(TC)
            dBu = AR.b(8 * TC)
            h8 = AR.b(8 * TC)
            hCb = AR.b(8 * TC)
            tr1 = h8[:, 0:4 * TC]
            tr2 = hCb[:, 0:2 * TC]
            dA2 = [AR.f(TC) for _ in range(2)]
            dA_free = [0, 0]
            sc4 = [AR.f(TC) for _ in range(4)]
            sc_free = [0, 0]
            ygr = [AR.f(TC) for _ in range(2)]
            yfull = AR.f(TC)

            banks.twait(4)
            P.add("t", _w(sS, u_t))
            for j in range(ICH):
                P.add("t", _mm(ps[4], c_xpj[:, j * 128:(j + 1) * 128],
                               u_bf[j], j == 0, j == ICH - 1,
                               sem=sT if j == ICH - 1 else None))
            P.add("s", _w(sT, sT.n))
            P.add("s", _act(dt_bf[0:64, :], ps[4][0:64, :], AF.Copy))
            P.add("s", _act(B_bf[0:16, :], ps[4][64:80, :], AF.Copy))
            P.add("s", _act(C_bf[0:16, :], ps[4][96:112, :], AF.Copy, sem=sS))
            banks.set_free(4, sS, sS.n)
            xdbl_t = sS.n

            for m in range(ICH):
                b0 = 4 + (m % 2)
                banks.twait(b0)
                if m == 0:
                    P.add("t", _w(sS, xdbl_t))
                P.add("t", _mm(ps[b0], c_dtp[:, m * 128:(m + 1) * 128],
                               dt_bf[0:64, :], True, True, sem=sT))
                dcol = c_dtb[:, m:m + 1]
                pp = m % 2
                s0, s1 = sc4[2 * pp], sc4[2 * pp + 1]
                P.add("s", _w(sT, sT.n))
                if sc_free[pp]:
                    P.add("s", _w(sV, sc_free[pp]))
                P.add("s", _act(s0, ps[b0], AF.Abs, bias=dcol))
                P.add("s", _act(s1, s0, AF.Exp, scale=-1.0))
                P.add("s", _act(s0, s1, AF.Ln, bias=1.0))
                P.add("s", _act(s1, ps[b0], AF.Relu, bias=dcol, sem=sS))
                st_ = sS.n
                banks.set_free(b0, sS, st_)
                P.add("v", _w(sS, st_))
                P.add("v", _tt(delta[m], s0, s1, OP.add, sem=sV))
                sc_free[pp] = sV.n
            delta_t = sV.n
            if t_delta is not None:
                P.add("g", _w(sV, delta_t))
                for j in range(ICH):
                    qG.dma(t_delta[j * 128:(j + 1) * 128, csl],
                                    delta[j])

            for g in range(2):
                for srcrow, dstbc in ((B_bf, B_bc[g]), (C_bf, C_bc[g])):
                    for i in range(8):
                        n = g * 8 + i
                        bnk = 4 + (i % 2)
                        banks.twait(bnk)
                        if g == 0 and i == 0 and srcrow is B_bf:
                            P.add("t", _w(sS, xdbl_t))
                        P.add("t", _mm(ps[bnk],
                                       c_sel[:, n * 128:(n + 1) * 128],
                                       srcrow[0:16, 0:TC], True, True,
                                       sem=sT))
                        P.add("s", _w(sT, sT.n))
                        P.add("s", _act(dstbc[:, i * TC:(i + 1) * TC],
                                        ps[bnk], AF.Copy, sem=sS))
                        banks.set_free(bnk, sS, sS.n)
            bc_t = sS.n

            for j in range(ICH):
                P.add("v", _tt(du_b, delta[j], u_bf[j], OP.mult))
                for g in range(2):
                    du3 = _ap3(du_b, (0, 8), (1, TC))
                    bb3 = _ap3(B_bc[g], (TC, 8), (1, TC))
                    if j == 0 and g == 0:
                        P.add("v", _w(sS, bc_t))
                    P.add("v", _tt(_ap3(dBu, (TC, 8), (1, TC)), du3, bb3,
                                   OP.mult))
                    for i in range(8):
                        n = g * 8 + i
                        si = n % 2
                        if j == 0 and g == 0 and i == 0:
                            P.add("s", _w(sS, bc_t))
                        if dA_free[si]:
                            P.add("s", _w(sV, dA_free[si]))
                        P.add("s", _act(dA2[si], delta[j], AF.Exp,
                                        scale=c_A[:, j * 16 + n:j * 16 + n + 1],
                                        sem=sS))
                        P.add("v", _w(sS, sS.n))
                        P.add("v", _scan(h8[:, i * TC:(i + 1) * TC], dA2[si],
                                         dBu[:, i * TC:(i + 1) * TC],
                                         hcar[:, j * 16 + n:j * 16 + n + 1],
                                         sem=sV))
                        dA_free[si] = sV.n
                    hlast = bass.AP(h8.tensor, h8.offset + (TC - 1) * 1,
                                    [h8.ap[0], [TC, 8]])
                    P.add("v", _vcopy(
                        hcar[:, j * 16 + g * 8:j * 16 + g * 8 + 8], hlast))
                    P.add("v", _tt(hCb, h8, C_bc[g], OP.mult))
                    P.add("v", _tt(tr1, hCb[:, 0:4 * TC], hCb[:, 4 * TC:8 * TC],
                                   OP.add))
                    P.add("v", _tt(tr2, tr1[:, 0:2 * TC], tr1[:, 2 * TC:4 * TC],
                                   OP.add))
                    P.add("v", _tt(ygr[g], tr2[:, 0:TC], tr2[:, TC:2 * TC],
                                   OP.add))
                P.add("v", _stt(yfull, u_bf[j], c_D[:, j:j + 1], ygr[0],
                                OP.mult, OP.add))
                P.add("v", _tt(yfull, yfull, ygr[1], OP.add))
                P.add("v", _tt(y2[j], yfull, res_s[j], OP.mult,
                               sem=sV if j == ICH - 1 else None))
            y2_t = sV.n
            if t_y2 is not None:
                P.add("g", _w(sV, y2_t))
                for j in range(ICH):
                    qG.dma(t_y2[j * 128:(j + 1) * 128, csl], y2[j])

            AR.release(regmark)
            m_bf = AR.btiles(DCH, TC)
            g_bf = AR.btiles(FCH, TC)
            cch = AR.btiles(DCH, TC)
            c2s = AR.btiles(2, TC)
            c2_free = [0, 0]

            for m in range(DCH):
                src = outpT[:, m * 128:(m + 1) * 128] \
                    .rearrange("(k p) c -> p k c", p=128)
                sl, wbf, (csem, cval) = wpool.load(src, 2048, inner=128)
                b0 = (m % 2) * 2
                banks.twait(b0)
                P.add("t", _w(csem, cval))
                P.add("t", _w(sV, y2_t))
                for k in range(ICH):
                    P.add("t", _mm(ps[b0], wbf[:, k * 128:(k + 1) * 128],
                                   y2[k], k == 0, k == ICH - 1,
                                   sem=sT if k == ICH - 1 else None))
                tt = sT.n
                wpool.consumed(sl, sT, tt)
                P.add("s", _w(sT, tt))
                P.add("s", _act(m_bf[m], ps[b0], AF.Copy, sem=sS))
                banks.set_free(b0, sS, sS.n)
            m_t = sS.n
            if t_m is not None:
                P.add("g", _w(sS, m_t))
                for m in range(DCH):
                    qG.dma(t_m[m * 128:(m + 1) * 128, csl], m_bf[m])

            for mg in range(16):
                src = f1T[:, mg * 256:(mg + 1) * 256] \
                    .rearrange("(k p) c -> p k c", p=128)
                sl, wbf, (csem, cval) = wpool.load(src, 2048, inner=256)
                b0, b1_ = next_pair()
                banks.twait(b0)
                banks.twait(b1_)
                P.add("t", _w(csem, cval))
                P.add("t", _w(sS, m_t))
                for k in range(DCH):
                    P.add("t", _mm(ps[b0], wbf[:, k * 256:k * 256 + 128],
                                   m_bf[k], k == 0, k == DCH - 1))
                    P.add("t", _mm(ps[b1_], wbf[:, k * 256 + 128:k * 256 + 256],
                                   m_bf[k], k == 0, k == DCH - 1,
                                   sem=sT if k == DCH - 1 else None))
                tt = sT.n
                wpool.consumed(sl, sT, tt)
                P.add("s", _w(sT, tt))
                for pi, mi in ((b0, 2 * mg), (b1_, 2 * mg + 1)):
                    P.add("s", _act(g_bf[mi], ps[pi], AF.Gelu,
                                    bias=c_fb1[:, mi:mi + 1], sem=sS))
                banks.set_free(b0, sS, sS.n)
                banks.set_free(b1_, sS, sS.n)
            gg_t = sS.n

            for m in range(DCH):
                sls = []
                for hh in range(2):
                    src = f2T[hh * 2048:(hh + 1) * 2048, m * 128:(m + 1) * 128] \
                        .rearrange("(k p) c -> p k c", p=128)
                    sls.append(wpool.load(src, 2048, inner=128))
                b0 = (m % 2) * 2 + 1
                banks.twait(b0)
                P.add("t", _w(sls[0][2][0], sls[0][2][1]))
                P.add("t", _w(sls[1][2][0], sls[1][2][1]))
                P.add("t", _w(sS, gg_t))
                for k in range(FCH):
                    wbf = sls[k // 16][1]
                    P.add("t", _mm(ps[b0],
                                   wbf[:, (k % 16) * 128:(k % 16 + 1) * 128],
                                   g_bf[k], k == 0, k == FCH - 1,
                                   sem=sT if k == FCH - 1 else None))
                tt = sT.n
                for hh in range(2):
                    wpool.consumed(sls[hh][0], sT, tt)
                P.add("v", _w(sT, tt))
                P.add("v", _stt(cch[m], ps[b0], c_fb2[:, m:m + 1],
                                m_bf[m], OP.add, OP.add, sem=sV))
                banks.set_free(b0, sV, sV.n)
                cvt = sV.n
                qG.dma(c_sp[m * 128:(m + 1) * 128, csl], cch[m],
                       extra_wait=(sV, cvt))
                if c2_free[m % 2]:
                    P.add("s", _w(sT, c2_free[m % 2]))
                P.add("s", _w(sV, cvt))
                P.add("s", _act(c2s[m % 2], cch[m], AF.Square, sem=sS))
                ct = sS.n
                if m == 0:
                    banks.twait(4)
                    banks.twait(5)
                P.add("t", _w(sS, ct))
                P.add("t", _mm(ps[4][0:1, :], ones_col, cch[m],
                               m == 0, m == DCH - 1))
                P.add("t", _mm(ps[5][0:1, :], ones_col, c2s[m % 2],
                               m == 0, m == DCH - 1, sem=sT))
                c2_free[m % 2] = sT.n
            P.add("v", _w(sT, sT.n))
            P.add("v", _vcopy(sumc[:, csl], ps[4][0:1, :]))
            P.add("v", _vcopy(sumc2[:, csl], ps[5][0:1, :], sem=sV))
            banks.set_free(4, sV, sV.n)
            banks.set_free(5, sV, sV.n)

        chunk_barrier()
        AR.release(p1mark)
        qG.drain()
        n1z = AR.btiles(DCH, S)
        zmark = AR.mark()
        z0 = AR.ftiles(DCH, S)
        p1cm = AR.mark()

        P.add("v", _red(srow[0:1, 0:1], sumc, OP.add))
        P.add("v", _red(srow[0:1, 1:2], sumc2, OP.add))
        P.add("v", _memset(srow[0:1, 8:9], 0.0))
        P.add("v", _ts(srow[0:1, 0:1], srow[0:1, 0:1], 1.0 / (S * D), OP.mult))
        P.add("v", _ts(srow[0:1, 1:2], srow[0:1, 1:2], 1.0 / (S * D), OP.mult))
        P.add("v", _tt(srow[0:1, 2:3], srow[0:1, 0:1], srow[0:1, 0:1],
                       OP.mult))
        P.add("v", _tt(srow[0:1, 3:4], srow[0:1, 1:2], srow[0:1, 2:3],
                       OP.subtract))
        P.add("v", _ts(srow[0:1, 3:4], srow[0:1, 3:4], 1e-5, OP.add))
        P.add("v", _ts(srow[0:1, 3:4], srow[0:1, 3:4], 1e-5, OP.mult, sem=sV))

        rowsF = AR.f(S)
        mtok = rowsF[0:1, 0:S]
        vtok = rowsF[32:33, 0:S]
        rtok = rowsF[64:65, 0:S]
        mrepS = AR.f(S)
        rrepS = AR.f(S)
        q2r = mrepS[32:33, 0:S]
        epsY = rrepS[32:33, 0:1]
        P.add("v", _ts(mtok, sumc, 1.0 / D, OP.mult))
        P.add("v", _ts(vtok, sumc2, 1.0 / D, OP.mult))
        P.add("v", _tt(q2r, mtok, mtok, OP.mult))
        P.add("v", _tt(vtok, vtok, q2r, OP.subtract, sem=sV))
        P.add("s", _w(sV, sV.n))
        P.add("s", _act(epsY, srow[0:1, 3:4], AF.Copy, sem=sS))
        P.add("v", _w(sS, sS.n))
        P.add("v", _ts(vtok, vtok, epsY, OP.add, sem=sV))
        P.add("s", _w(sV, sV.n))
        P.add("s", _act(vtok, vtok, AF.Sqrt, sem=sS))
        P.add("v", _w(sS, sS.n))
        P.add("v", _recip(rtok, vtok))
        P.add("v", _ts(vtok, vtok, 1.0, OP.mult, sem=sV))
        tok_t = sV.n

        rowsB = AR.b(S)
        rows2 = [rowsB[0:1, :], rowsB[32:33, :]]
        P.add("s", _w(sV, tok_t))
        P.add("s", _act(rows2[0], mtok, AF.Copy))
        P.add("s", _act(rows2[1], rtok, AF.Copy, sem=sS))
        P.add("t", _w(sS, sS.n))
        replicate_row(rows2[0], mrepS, 4)
        replicate_row(rows2[1], rrepS, 5, bp=32)
        rep_t = sS.n

        x0r = AR.f(S)
        x1b = AR.f(S)
        t1b = AR.f(S)
        crel = AR.b(S)
        sh2 = [AR.b(S) for _ in range(2)]
        P.add("v", _w(sS, rep_t))
        prev_v = 0
        for m in range(DCH):
            ew = (sV, prev_v) if prev_v else None
            t1_tk = qG.dma(x0r, x0T[m * 128:(m + 1) * 128, :], extra_wait=ew)
            t2_tk = qG.dma(crel, c_sp[m * 128:(m + 1) * 128, :])
            P.add("v", _w(*t1_tk))
            P.add("v", _w(*t2_tk))
            P.add("v", _tt(t1b, crel, mrepS, OP.subtract))
            P.add("v", _tt(t1b, t1b, rrepS, OP.mult))
            P.add("v", _tt(x1b, x0r, t1b, OP.add, sem=sV))
            vt = sV.n
            sp_tk = qG.dma(x1sp[m * 128:(m + 1) * 128, :], x1b,
                           extra_wait=(sV, vt))
            P.add("s", _w(sV, vt))
            P.add("s", _act(sh2[0], x1b, AF.Copy))
            P.add("s", _act(sh2[1], x1b, AF.Square, sem=sS))
            ct = sS.n
            banks.twait(4)
            banks.twait(5)
            P.add("t", _w(sS, ct))
            for i in range(4):
                isl = slice(i * 512, (i + 1) * 512)
                if i > 0:
                    P.add("t", _w(sV, sV.n))
                P.add("t", _mm(ps[4][0:1, :], ones_col, sh2[0][:, isl],
                               True, True))
                P.add("t", _mm(ps[5][0:1, :], ones_col, sh2[1][:, isl],
                               True, True, sem=sT))
                P.add("v", _w(sT, sT.n))
                P.add("v", _tt(sumx[:, isl], sumx[:, isl], ps[4][0:1, :],
                               OP.add))
                P.add("v", _tt(sumx2[:, isl], sumx2[:, isl],
                               ps[5][0:1, :], OP.add, sem=sV))
            banks.set_free(4, sV, sV.n)
            banks.set_free(5, sV, sV.n)
            P.add("v", _w(*sp_tk))
            prev_v = sV.n
        x1_t = prev_v

        P.add("v", _ts(mtok, sumx, 1.0 / D, OP.mult))
        P.add("v", _ts(vtok, sumx2, 1.0 / D, OP.mult))
        P.add("v", _tt(q2r, mtok, mtok, OP.mult))
        P.add("v", _tt(vtok, vtok, q2r, OP.subtract, sem=sV))
        P.add("s", _w(sV, sV.n))
        P.add("s", _act(vtok, vtok, AF.Sqrt, bias=c_eps[32:33, 0:1], sem=sS))
        P.add("v", _w(sS, sS.n))
        P.add("v", _recip(rtok, vtok))
        P.add("v", _ts(vtok, vtok, 1.0, OP.mult, sem=sV))
        P.add("s", _w(sV, sV.n))
        P.add("s", _act(rows2[0], mtok, AF.Copy))
        P.add("s", _act(rows2[1], rtok, AF.Copy, sem=sS))
        P.add("t", _w(sS, sS.n))
        replicate_row(rows2[0], mrepS, 4)
        replicate_row(rows2[1], rrepS, 5, bp=32)
        rep2_t = sS.n

        x1r2 = x0r
        t_z0 = tap("z0", [D, S])
        prev_v = 0
        for m in range(DCH):
            ew = (sV, x1_t if not prev_v else prev_v)
            r_tk = qG.dma(x1r2, x1sp[m * 128:(m + 1) * 128, :], extra_wait=ew)
            P.add("v", _w(*r_tk))
            if m == 0:
                P.add("v", _w(sS, rep2_t))
            P.add("v", _tt(z0[m], x1r2, mrepS, OP.subtract))
            P.add("v", _tt(z0[m], z0[m], rrepS, OP.mult, sem=sV))
            prev_v = sV.n
            if t_z0 is not None:
                P.add("g", _w(sV, prev_v))
                qG.dma(t_z0[m * 128:(m + 1) * 128, :], z0[m])
        z0_t = prev_v

        if upto != "p1":
            chunk_barrier()
            AR.release(p1cm)
            p2m = AR.mark()

            shz2 = [AR.b(512) for _ in range(2)]
            shz_fr = [0, 0]
            msr = AR.f(S)[0:1, :]
            rsrep = AR.f(S)
            rowsb = AR.b(S)
            for sl in range(4):
                isl = slice(sl * 512, (sl + 1) * 512)
                if sl > 0:
                    P.add("t", _w(sV, sV.n))
                for och in range(DCH):
                    sp = och % 2
                    if shz_fr[sp]:
                        P.add("s", _w(sT, shz_fr[sp]))
                    P.add("s", _act(shz2[sp], z0[och][:, isl],
                                    AF.Square, sem=sS))
                    if och == 0 and sl == 0:
                        banks.twait(4)
                    P.add("t", _w(sS, sS.n))
                    P.add("t", _mm(ps[4][0:1, :], ones_col, shz2[sp],
                                   och == 0, och == DCH - 1, sem=sT))
                    shz_fr[sp] = sT.n
                P.add("v", _w(sT, sT.n))
                P.add("v", _ts(msr[:, isl], ps[4][0:1, :], 1.0 / D, OP.mult,
                               sem=sV))
            banks.set_free(4, sV, sV.n)
            P.add("v", _ts(msr, msr, c_eps[0:1, 1:2], OP.add, sem=sV))
            P.add("s", _w(sV, sV.n))
            P.add("s", _act(msr, msr, AF.Sqrt, sem=sS))
            P.add("v", _w(sS, sS.n))
            P.add("v", _recip(msr, msr))
            P.add("v", _memset(srow[0:1, 9:10], 0.0, sem=sV))
            P.add("s", _w(sV, sV.n))
            P.add("s", _act(rowsb[0:1, :], msr, AF.Copy, sem=sS))
            P.add("t", _w(sS, sS.n))
            replicate_row(rowsb[0:1, :], rsrep, 4)
            P.add("v", _w(sS, sS.n))
            n1t = 0
            for och in range(DCH):
                P.add("v", _stt(n1z[och], z0[och], c_n1w[:, och:och + 1],
                                rsrep, OP.mult, OP.mult,
                                sem=sV if och == DCH - 1 else None))
            n1t = sV.n
            for och in range(DCH):
                qG.dma(z0sp[och * 128:(och + 1) * 128, :], z0[och],
                       extra_wait=(sV, n1t) if och == 0 else None)
            qG.drain()
            chunk_barrier()

            AR.release(zmark)
            c_cos = AR.b(S)
            c_sin = AR.b(S)
            cos_tk = qG.dma(c_cos, cosTd)
            sin_tk = qG.dma(c_sin, sinTd)
            atile = AR.btiles(2, S)
            atile_fr = [None, None]
            at_spill = [None] * 8
            v_aug = AR.btiles(16, 16 * 65)
            qb = AR.btiles(4, S)
            kb = AR.btiles(4, S)
            q2b = [AR.b(512) for _ in range(2)]
            q2fr = [0, 0]
            qnkn = AR.f(S)
            qn = qnkn[0:8, :]
            kn = qnkn[32:40, :]
            mtl = qnkn[64:72, :]
            kmx = AR.f(16)
            mtl_bf = AR.b(S)
            mrep4 = [AR.f(512) for _ in range(4)]
            sprm = [AR.f(512) for _ in range(2)]
            sprm_fr = [0, 0]
            ptil = [AR.b(512) for _ in range(2)]
            ptil_fr = [0, 0]
            brep = AR.f(512)
            brep_fr = [0]
            rbuf = AR.f(512)
            rrw = AR.f(512)
            vwb = AR.b(8 * 512)
            t1b_ = AR.b(512)
            t2b_ = AR.f(512)

            for tt in range(16):
                P.add("v", _memset(_ap3(v_aug[tt][:, 64:], (65, 16), (1, 1)),
                                   1.0, sem=sV if tt == 15 else None))
            vones_t = sV.n

            for H in range(2):
                base_q, base_k, base_v = H * 512, D + H * 512, 2 * D + H * 512
                for src0, dstt, bc0 in ((base_q, qb, H * 4),
                                        (base_k, kb, 8 + H * 4)):
                    for mt in range(4):
                        wsrc = qkvT[:, src0 + mt * 128:src0 + (mt + 1) * 128] \
                            .rearrange("(k p) c -> p k c", p=128)
                        sl_, wbf, tk = wpool.load(wsrc, 1024, inner=128)
                        for n4 in range(4):
                            nsl = slice(n4 * 512, (n4 + 1) * 512)
                            b0 = n4 % 2
                            banks.twait(b0)
                            if n4 == 0:
                                P.add("t", _w(*tk))
                            for k in range(DCH):
                                P.add("t", _mm(ps[b0],
                                               wbf[:, k * 128:(k + 1) * 128],
                                               n1z[k][:, nsl], k == 0,
                                               k == DCH - 1,
                                               sem=sT if k == DCH - 1 else None))
                            tt_ = sT.n
                            if n4 == 3:
                                wpool.consumed(sl_, sT, tt_)
                            P.add("s", _w(sT, tt_))
                            P.add("s", _act(dstt[mt][:, nsl], ps[b0],
                                            AF.Identity,
                                            bias=c_qkvb[:, bc0 + mt:bc0 + mt + 1],
                                            sem=sS))
                            banks.set_free(b0, sS, sS.n)
                t_kpre = tap(f"kpre{H}", [512, S])
                if t_kpre is not None:
                    P.add("g", _w(sS, sS.n))
                    for it in range(4):
                        qG.dma(t_kpre[it * 128:(it + 1) * 128, :], kb[it])
                if H == 0:
                    P.add("v", _w(*cos_tk))
                    P.add("v", _w(*sin_tk))
                for tile in list(qb) + list(kb):
                    for n4 in range(4):
                        nsl = slice(n4 * 512, (n4 + 1) * 512)
                        banks.twait(6)
                        P.add("t", _w(sS, sS.n))
                        P.add("t", _mm(ps[6], c_rotP[:], tile[:, nsl], True, True,
                                       sem=sT))
                        P.add("v", _w(sT, sT.n))
                        P.add("v", _tt(t1b_, tile[:, nsl], c_cos[:, nsl],
                                       OP.mult))
                        P.add("v", _tt(t2b_, ps[6], c_sin[:, nsl], OP.mult))
                        P.add("v", _tt(tile[:, nsl], t1b_, t2b_, OP.add,
                                       sem=sV))
                        banks.set_free(6, sV, sV.n)
                rope_t = sV.n
                vw = qkvT[:, base_v:base_v + 512] \
                    .rearrange("(k p) c -> p k c", p=128)
                vtk = qG.dma(vwb.rearrange("p (k c) -> p k c", c=512), vw,
                             extra_wait=(sT, vmm_t) if H else None)
                bvsl = c_bv[0:1, H * 512:(H + 1) * 512]
                if H:
                    P.add("s", _w(sV, brep_fr[0]))
                P.add("s", _act(rowsb[0:1, 0:512], bvsl, AF.Copy, sem=sS))
                P.add("t", _w(sS, sS.n))
                replicate_row(rowsb[0:1, 0:512], brep, 5)
                brt = sS.n
                for tt in range(16):
                    b0 = 2 + tt % 2
                    banks.twait(b0)
                    if tt == 0:
                        P.add("t", _w(*vtk))
                        P.add("t", _w(sV, vones_t))
                    for k in range(DCH):
                        P.add("t", _mm(ps[b0], n1z[k][:, tt * 128:(tt + 1) * 128],
                                       vwb[:, k * 512:(k + 1) * 512], k == 0,
                                       k == DCH - 1,
                                       sem=sT if k == DCH - 1 else None))
                    P.add("v", _w(sT, sT.n))
                    if tt == 0:
                        P.add("v", _w(sS, brt))
                    vdst = bass.AP(v_aug[tt].tensor,
                                   v_aug[tt].offset + H * 8 * 65,
                                   [v_aug[tt].ap[0], [65, 8], [1, 64]])
                    P.add("v", _tt(vdst, ps[b0], brep, OP.add, sem=sV))
                    banks.set_free(b0, sV, sV.n)
                    if tt == 15:
                        vmm_t = sT.n
                        brep_fr[0] = sV.n
                for srct, dstn in ((qb, qn), (kb, kn)):
                    for qsl in range(4):
                        nsl = slice(qsl * 512, (qsl + 1) * 512)
                        if not (srct is qb and qsl == 0):
                            P.add("t", _w(sV, sV.n))
                        for i in range(4):
                            sp = i % 2
                            if q2fr[sp]:
                                P.add("s", _w(sT, q2fr[sp]))
                            if i == 0 and qsl == 0 and srct is qb:
                                P.add("s", _w(sV, rope_t))
                            P.add("s", _act(q2b[sp], srct[i][:, nsl],
                                            AF.Square, sem=sS))
                            banks.twait(6)
                            P.add("t", _w(sS, sS.n))
                            P.add("t", _mm(ps[6][0:8, :],
                                           c_hbq[:, i * 8:(i + 1) * 8],
                                           q2b[sp], i == 0, i == 3,
                                           sem=sT))
                            q2fr[sp] = sT.n
                        P.add("v", _w(sT, sT.n))
                        P.add("v", _vcopy(dstn[:, nsl], ps[6][0:8, :],
                                          sem=sV))
                        banks.set_free(6, sV, sV.n)
                P.add("v", _red(kmx[0:8, 0:1], kn, OP.max))
                P.add("v", _memset(srow[0:1, 10:11], 0.0, sem=sV))
                P.add("s", _w(sV, sV.n))
                P.add("s", _act(kmx[0:8, 0:1], kmx[0:8, 0:1], AF.Sqrt))
                P.add("s", _act(qn, qn, AF.Sqrt, sem=sS))
                P.add("v", _w(sS, sS.n))
                P.add("v", _ts(mtl, qn, kmx[0:8, 0:1],
                               OP.mult, sem=sV))
                P.add("s", _w(sV, sV.n))
                P.add("s", _act(mtl_bf[0:8, :], mtl, AF.Copy, sem=sS))
                mtl_t = sS.n
                t_q = tap(f"qx{H}", [512, S])
                t_k = tap(f"kx{H}", [512, S])
                if t_q is not None:
                    for it in range(4):
                        qG.dma(t_q[it * 128:(it + 1) * 128, :], qb[it],
                               extra_wait=(sV, rope_t))
                        if t_k is not None:
                            qG.dma(t_k[it * 128:(it + 1) * 128, :], kb[it])
                t_mt = tap(f"mt{H}", [8, S])
                if t_mt is not None:
                    P.add("v", _w(sS, mtl_t))
                    P.add("v", _vcopy(qnkn[96:104, :], mtl, sem=sV))
                    qG.dma(t_mt, qnkn[96:104, :], extra_wait=(sV, sV.n))

                for hh in range(8):
                    hg = H * 8 + hh
                    ti, ro = hh // 2, (hh % 2) * 64
                    for qsl in range(4):
                        banks.twait(6)
                        if hh == 0 and qsl == 0:
                            P.add("t", _w(sS, mtl_t))
                        P.add("t", _mm(ps[6],
                                       c_sel[0:8, hh * 128:(hh + 1) * 128],
                                       mtl_bf[0:8, qsl * 512:(qsl + 1) * 512],
                                       True, True, sem=sT))
                        P.add("s", _w(sT, sT.n))
                        P.add("s", _act(mrep4[qsl], ps[6], AF.Copy, sem=sS))
                        banks.set_free(6, sS, sS.n)
                    mrt = sS.n
                    pq = []

                    def _flush_acc():
                        kt_p, qsl_p, sp_p, exp_t = pq.pop(0)
                        P.add("t", _w(sS, exp_t))
                        P.add("t", _mm(ps[2 + qsl_p][0:65, :],
                                       v_aug[kt_p][:, hg * 65:hg * 65 + 65],
                                       ptil[sp_p], kt_p == 0, kt_p == 15,
                                       sem=sT))
                        ptil_fr[sp_p] = sT.n

                    for kt in range(16):
                        for qsl in range(4):
                            ib = (kt * 4 + qsl)
                            b0 = ib % 2
                            sp = ib % 2
                            banks.twait(b0)
                            P.add("t", _mm(ps[b0],
                                           kb[ti][ro:ro + 64,
                                                  kt * 128:(kt + 1) * 128],
                                           qb[ti][ro:ro + 64,
                                                  qsl * 512:(qsl + 1) * 512],
                                           True, True, sem=sT))
                            sc_t = sT.n
                            if len(pq) >= 2:
                                _flush_acc()
                            if sprm_fr[sp]:
                                P.add("v", _w(sS, sprm_fr[sp]))
                            P.add("v", _w(sT, sc_t))
                            if kt == 0 and qsl == 0:
                                P.add("v", _w(sS, mrt))
                            P.add("v", _tt(sprm[sp], ps[b0], mrep4[qsl],
                                           OP.subtract, sem=sV))
                            banks.set_free(b0, sV, sV.n)
                            if ptil_fr[sp]:
                                P.add("s", _w(sT, ptil_fr[sp]))
                            P.add("s", _w(sV, sV.n))
                            P.add("s", _act(ptil[sp], sprm[sp], AF.Exp,
                                            sem=sS))
                            pq.append((kt, qsl, sp, sS.n))
                            sprm_fr[sp] = sS.n
                    while pq:
                        _flush_acc()
                    acc_t = sT.n
                    for qsl in range(4):
                        P.add("v", _w(sT, acc_t))
                        P.add("v", _recip(rbuf[0:1, :],
                                          ps[2 + qsl][64:65, :]))
                        P.add("v", _memset(srow[0:1, 11:12], 0.0, sem=sV))
                        P.add("s", _w(sV, sV.n))
                        P.add("s", _act(rowsb[0:1, 0:512], rbuf[0:1, :],
                                        AF.Copy, sem=sS))
                        banks.twait(6)
                        P.add("t", _w(sS, sS.n))
                        P.add("t", _mm(ps[6][0:64, :], c_ones[0:1, 0:64],
                                       rowsb[0:1, 0:512], True, True, sem=sT))
                        P.add("s", _w(sT, sT.n))
                        P.add("s", _act(rrw[0:64, :], ps[6][0:64, :], AF.Copy,
                                        sem=sS))
                        banks.set_free(6, sS, sS.n)
                        P.add("v", _w(sS, sS.n))
                        pslot = (hh // 2) % 2
                        if qsl == 0 and ro == 0 and atile_fr[pslot]:
                            P.add("v", _w(*atile_fr[pslot]))
                        P.add("v", _tt(atile[pslot][ro:ro + 64,
                                                    qsl * 512:(qsl + 1) * 512],
                                       ps[2 + qsl][0:64, :], rrw[0:64, :],
                                       OP.mult, sem=sV))
                        banks.set_free(2 + qsl, sV, sV.n)
                    at_t = sV.n
                    if hh % 2 == 1:
                        pr = hg // 2
                        pslot = (hh // 2) % 2
                        atile_fr[pslot] = qG.dma(
                            atsp[pr * 128:(pr + 1) * 128, :], atile[pslot],
                            extra_wait=(sV, at_t))
                        at_spill[pr] = atile_fr[pslot]
            chunk_barrier()

            AR.release(p1mark)
            z = AR.ftiles(DCH, S)
            zend = AR.mark()
            moemark = zend
            attn = AR.btiles(DCH, S)
            atk = [qG.dma(attn[k], atsp[k * 128:(k + 1) * 128, :],
                          extra_wait=at_spill[k])
                   for k in range(DCH)]
            for tkk in atk:
                P.add("t", _w(*tkk))
            t_atr = tap("atr", [D, S])
            if t_atr is not None:
                for k in range(DCH):
                    qG.dma(t_atr[k * 128:(k + 1) * 128, :], attn[k])
            for och in range(DCH):
                wsrc = outwT[:, och * 128:(och + 1) * 128] \
                    .rearrange("(k p) c -> p k c", p=128)
                sl_, wbf, tk = wpool.load(wsrc, 1024, inner=128)
                ztk = qG.dma(z[och], z0sp[och * 128:(och + 1) * 128, :])
                for n4 in range(4):
                    nsl = slice(n4 * 512, (n4 + 1) * 512)
                    b0 = n4 % 2
                    banks.twait(b0)
                    if n4 == 0:
                        P.add("t", _w(*tk))
                    for k in range(DCH):
                        P.add("t", _mm(ps[b0], wbf[:, k * 128:(k + 1) * 128],
                                       attn[k][:, nsl], k == 0, k == DCH - 1,
                                       sem=sT if k == DCH - 1 else None))
                    tt_ = sT.n
                    if n4 == 3:
                        wpool.consumed(sl_, sT, tt_)
                    P.add("v", _w(sT, tt_))
                    if n4 == 0:
                        P.add("v", _w(*ztk))
                    P.add("v", _stt(z[och][:, nsl], ps[b0],
                                    c_outb[:, och:och + 1], z[och][:, nsl],
                                    OP.add, OP.add, sem=sV))
                    banks.set_free(b0, sV, sV.n)
            z1_t = sV.n
            t_z1 = tap("z1", [D, S])
            if t_z1 is not None:
                for och in range(DCH):
                    qG.dma(t_z1[och * 128:(och + 1) * 128, :], z[och],
                           extra_wait=(sV, z1_t))
            chunk_barrier()

            AR.release(moemark)
            n2 = AR.btiles(DCH, S)
            shz2 = [AR.b(512) for _ in range(2)]
            shz_fr = [0, 0]
            msrT = AR.f(S)
            msr = msrT[0:1, :]
            rowsb = msrT[32:33, :].bitcast(BF)[0:1, 0:S]
            rsrep = AR.f(S)
            for sl in range(4):
                isl = slice(sl * 512, (sl + 1) * 512)
                if sl > 0:
                    P.add("t", _w(sV, sV.n))
                for och in range(DCH):
                    sp = och % 2
                    if shz_fr[sp]:
                        P.add("s", _w(sT, shz_fr[sp]))
                    P.add("s", _act(shz2[sp], z[och][:, isl],
                                    AF.Square, sem=sS))
                    if och == 0 and sl == 0:
                        banks.twait(4)
                    P.add("t", _w(sS, sS.n))
                    P.add("t", _mm(ps[4][0:1, :], ones_col, shz2[sp],
                                   och == 0, och == DCH - 1, sem=sT))
                    shz_fr[sp] = sT.n
                P.add("v", _w(sT, sT.n))
                P.add("v", _ts(msr[:, isl], ps[4][0:1, :], 1.0 / D, OP.mult,
                               sem=sV))
            banks.set_free(4, sV, sV.n)
            P.add("v", _ts(msr, msr, c_eps[0:1, 1:2], OP.add, sem=sV))
            P.add("s", _w(sV, sV.n))
            P.add("s", _act(msr, msr, AF.Sqrt, sem=sS))
            P.add("v", _w(sS, sS.n))
            P.add("v", _recip(msr, msr))
            P.add("v", _memset(srow[0:1, 9:10], 0.0, sem=sV))
            P.add("s", _w(sV, sV.n))
            P.add("s", _act(rowsb[0:1, :], msr, AF.Copy, sem=sS))
            P.add("t", _w(sS, sS.n))
            replicate_row(rowsb[0:1, :], rsrep, 4, bp=32)
            P.add("v", _w(sS, sS.n))
            for och in range(DCH):
                P.add("v", _stt(n2[och], z[och], c_n2w[:, och:och + 1],
                                rsrep, OP.mult, OP.mult,
                                sem=sV if och == DCH - 1 else None))
            n2t = sV.n
            t_n2 = tap("n2", [D, S])
            if t_n2 is not None:
                for och in range(DCH):
                    qG.dma(t_n2[och * 128:(och + 1) * 128, :], n2[och],
                           extra_wait=(sV, n2t))

            lgT = AR.f(128)
            mxa = AR.f(128)
            dsc = AR.f(8)
            e1b = AR.f(8)
            e2b = AR.f(8)
            mskb = AR.f(8)
            ddc = AR.f(16)
            p1c_ = AR.f(16)
            p2c_ = AR.f(16)
            combf = AR.b(128)
            combT = AR.b(S)
            for tt in range(16):
                b0 = 4 + tt % 2
                banks.twait(b0)
                if tt == 0:
                    P.add("t", _w(sV, n2t))
                for k in range(DCH):
                    P.add("t", _mm(ps[b0][:, 0:8],
                                   n2[k][:, tt * 128:(tt + 1) * 128],
                                   c_gw[:, k * 8:(k + 1) * 8],
                                   k == 0, k == DCH - 1,
                                   sem=sT if k == DCH - 1 else None))
                P.add("v", _w(sT, sT.n))
                P.add("v", _vcopy(lgT[:, tt * 8:(tt + 1) * 8], ps[b0][:, 0:8],
                                  sem=sV))
                banks.set_free(b0, sV, sV.n)
            t_lg = tap("lgt", [128, 128])
            if t_lg is not None:
                P.add("v", _ts(srow[0:1, 14:15], srow[0:1, 14:15], 0.0,
                               OP.mult, sem=sV))
                qG.dma(t_lg, lgT, extra_wait=(sV, sV.n))
            for tt in range(16):
                P.add("v", _max8(mxa[:, tt * 8:(tt + 1) * 8],
                                 lgT[:, tt * 8:(tt + 1) * 8]))
            P.add("v", _memset(srow[0:1, 12:13], 0.0, sem=sV))
            gjnk = AR.f(8)
            for tt in range(16):
                lgt = lgT[:, tt * 8:(tt + 1) * 8]
                v1 = mxa[:, tt * 8:tt * 8 + 1]
                v2 = mxa[:, tt * 8 + 1:tt * 8 + 2]
                v1b = bass.AP(v1.tensor, v1.offset, [v1.ap[0], [0, 8]])
                v2b = bass.AP(v2.tensor, v2.offset, [v2.ap[0], [0, 8]])
                P.add("v", _tt(dsc, v1b, lgt, OP.subtract))
                P.add("v", _tt(ddc[:, tt:tt + 1], v1, v2, OP.subtract,
                               sem=sV))
                P.add("s", _w(sV, sV.n))
                P.add("s", _act(p1c_[:, tt:tt + 1], ddc[:, tt:tt + 1],
                                AF.Sigmoid))
                P.add("s", _act(p2c_[:, tt:tt + 1], ddc[:, tt:tt + 1],
                                AF.Sigmoid, scale=-1.0, sem=sS))
                P.add("v", _ts(e1b, dsc, -1.0, OP.mult, -1e-20, OP.is_ge))
                if tt == 0:
                    t_d1 = tap("d1t", [128, 8])
                    t_e1 = tap("e1t", [128, 8])
                    if t_d1 is not None:
                        P.add("v", _ts(srow[0:1, 15:16], srow[0:1, 15:16],
                                       0.0, OP.mult, sem=sV))
                        qG.dma(t_d1, dsc, extra_wait=(sV, sV.n))
                        qG.dma(t_e1, e1b)
                        P.add("v", _w(*qG.dma(tap("v1t", [128, 8]) or t_d1,
                                              mxa[:, 0:8])))
                P.add("v", _ts(gjnk, lgt, 1.0, OP.mult))
                P.add("v", _stt(mskb, e1b, -1e30, lgt, OP.mult, OP.add))
                P.add("v", _ts(gjnk, lgt, 1.0, OP.mult))
                P.add("v", _tt(dsc, v2b, mskb, OP.subtract))
                P.add("v", _w(sS, sS.n))
                P.add("v", _ts(combf[:, tt * 8:(tt + 1) * 8], e1b,
                               p1c_[:, tt:tt + 1], OP.mult))
                P.add("v", _ts(e2b, dsc, -1.0, OP.mult, -1e-20, OP.is_ge))
                P.add("v", _ts(gjnk, lgt, 1.0, OP.mult))
                P.add("v", _stt(combf[:, tt * 8:(tt + 1) * 8], e2b,
                                p2c_[:, tt:tt + 1],
                                combf[:, tt * 8:(tt + 1) * 8], OP.mult,
                                OP.add, sem=sV))
            comb_t = sV.n
            t_mx = tap("mxa", [128, 128])
            if t_mx is not None:
                qG.dma(t_mx, mxa, extra_wait=(sV, comb_t))
            t_dd = tap("ddt", [128, 16])
            if t_dd is not None:
                qG.dma(t_dd, ddc, extra_wait=(sV, comb_t))
                qG.dma(tap("p1t", [128, 16]), p1c_, extra_wait=(sV, comb_t))
                qG.dma(tap("p2t", [128, 16]), p2c_, extra_wait=(sV, comb_t))
            t_cf = tap("combf", [128, 128])
            if t_cf is not None:
                qG.dma(t_cf, combf, extra_wait=(sV, comb_t))
            for rnd in range(2):
                P.add("t", _w(sV, comb_t if rnd == 0 else sV.n))
                for q in range(8):
                    tt = rnd * 8 + q
                    P.add("t", _trans(
                        bass.AP(psbf.tensor, psbf.offset + q * 128,
                                [[psbf.ap[0][0], 8], [1, 128]]),
                        combf[:, tt * 8:(tt + 1) * 8], c_ident[:, 0:128],
                        sem=sT if q == 7 else None))
                P.add("v", _w(sT, sT.n))
                for q in range(8):
                    tt = rnd * 8 + q
                    P.add("v", _vcopy(combT[0:8, tt * 128:(tt + 1) * 128],
                                      psbf[0:8, q * 128:(q + 1) * 128],
                                      sem=sV if q == 7 else None))
            combT_t = sV.n
            t_comb = tap("comb", [8, S])
            if t_comb is not None:
                qG.dma(t_comb, combT[0:8, :], extra_wait=(sV, combT_t))
            chunk_barrier()

            moem2 = AR.mark()
            h_bf = AR.btiles(MCH, S // 2)
            crep = AR.b(S // 2)
            crep_fr = [0]
            sab = [AR.b(512) for _ in range(2)]
            sbb2 = [AR.b(512) for _ in range(2)]
            sab_fr = [0, 0]
            tbb = [AR.f(512) for _ in range(2)]
            tbb_fr = [0, 0]
            for th in range(2):
                ts0 = th * (S // 2)
                for e in range(NE):
                    for q2_ in range(2):
                        banks.twait(6)
                        if e == 0 and q2_ == 0 and th == 0:
                            P.add("t", _w(sV, combT_t))
                        P.add("t", _mm(ps[6],
                                       c_sel[0:8, e * 128:(e + 1) * 128],
                                       combT[0:8, ts0 + q2_ * 512:
                                             ts0 + (q2_ + 1) * 512],
                                       True, True, sem=sT))
                        P.add("s", _w(sT, sT.n))
                        if crep_fr[0]:
                            P.add("s", _w(sV, crep_fr[0]))
                        P.add("s", _act(crep[:, q2_ * 512:(q2_ + 1) * 512],
                                        ps[6], AF.Copy, sem=sS))
                        banks.set_free(6, sS, sS.n)
                    crt = sS.n
                    if th == 0 and e == 0:
                        tba = tap("cba", [8, S])
                        if tba is not None:
                            dtk = qG.dma(tba, combT[0:8, :],
                                         extra_wait=(sS, crt))
                            P.add("s", _w(*dtk))
                    for m in range(MCH):
                        w1src = w1T[e][:, m * 128:(m + 1) * 128] \
                            .rearrange("(k p) c -> p k c", p=128)
                        w2src = w2T[e][:, m * 128:(m + 1) * 128] \
                            .rearrange("(k p) c -> p k c", p=128)
                        sl1, wb1, tk1 = wpool.load(w1src, 1024, inner=128)
                        sl2, wb2, tk2 = wpool.load(w2src, 1024, inner=128)
                        for n2_ in range(2):
                            nsl = slice(ts0 + n2_ * 512, ts0 + (n2_ + 1) * 512)
                            banks.twait(0)
                            banks.twait(1)
                            if n2_ == 0:
                                P.add("t", _w(*tk1))
                                P.add("t", _w(*tk2))
                            for k in range(DCH):
                                P.add("t", _mm(ps[0],
                                               wb1[:, k * 128:(k + 1) * 128],
                                               n2[k][:, nsl], k == 0,
                                               k == DCH - 1))
                                P.add("t", _mm(ps[1],
                                               wb2[:, k * 128:(k + 1) * 128],
                                               n2[k][:, nsl], k == 0,
                                               k == DCH - 1,
                                               sem=sT if k == DCH - 1 else None))
                            tt_ = sT.n
                            if n2_ == 1:
                                wpool.consumed(sl1, sT, tt_)
                                wpool.consumed(sl2, sT, tt_)
                            sp = n2_ % 2
                            bcol = e * 16 + m
                            if sab_fr[sp]:
                                P.add("s", _w(sV, sab_fr[sp]))
                            P.add("s", _w(sT, tt_))
                            P.add("s", _act(sab[sp], ps[0], AF.Silu,
                                            bias=c_b1[:, bcol:bcol + 1]))
                            P.add("s", _act(sbb2[sp], ps[1], AF.Identity,
                                            bias=c_b2[:, bcol:bcol + 1],
                                            sem=sS))
                            st_ = sS.n
                            banks.set_free(0, sS, st_)
                            banks.set_free(1, sS, st_)
                            P.add("v", _w(sS, st_))
                            P.add("v", _tt(h_bf[m][:, n2_ * 512:
                                                   (n2_ + 1) * 512],
                                           sab[sp], sbb2[sp], OP.mult,
                                           sem=sV))
                            sab_fr[sp] = sV.n
                    h_t = sV.n
                    if th == 0 and e == 0:
                        tbb_ = tap("cbb", [8, S])
                        if tbb_ is not None:
                            dtk = qG.dma(tbb_, combT[0:8, :],
                                         extra_wait=(sV, h_t))
                            P.add("s", _w(*dtk))
                            P.add("v", _w(*dtk))
                    for och in range(DCH):
                        w3src = w3T[e][:, och * 128:(och + 1) * 128] \
                            .rearrange("(k p) c -> p k c", p=128)
                        sl3, wb3, tk3 = wpool.load(w3src, 2048, inner=128)
                        for n2_ in range(2):
                            nsl_h = slice(n2_ * 512, (n2_ + 1) * 512)
                            nsl_z = slice(ts0 + n2_ * 512,
                                          ts0 + (n2_ + 1) * 512)
                            b0 = 2 + (n2_ % 2)
                            banks.twait(b0)
                            if n2_ == 0:
                                P.add("t", _w(*tk3))
                                P.add("t", _w(sV, h_t))
                            for k in range(MCH):
                                P.add("t", _mm(ps[b0],
                                               wb3[:, k * 128:(k + 1) * 128],
                                               h_bf[k][:, nsl_h], k == 0,
                                               k == MCH - 1,
                                               sem=sT if k == MCH - 1
                                               else None))
                            tt_ = sT.n
                            if n2_ == 1:
                                wpool.consumed(sl3, sT, tt_)
                            sp = n2_ % 2
                            bcol = e * 8 + och
                            if tbb_fr[sp]:
                                P.add("v", _w(sV, tbb_fr[sp]))
                            P.add("v", _w(sT, tt_))
                            if och == 0 and n2_ == 0:
                                P.add("v", _w(sS, crt))
                            P.add("v", _stt(tbb[sp], ps[b0],
                                            c_b3[:, bcol:bcol + 1],
                                            crep[:, nsl_h], OP.add, OP.mult))
                            P.add("v", _tt(z[och][:, nsl_z], z[och][:, nsl_z],
                                           tbb[sp], OP.add, sem=sV))
                            tbb_fr[sp] = sV.n
                            banks.set_free(b0, sV, sV.n)
                            crep_fr[0] = sV.n
                    if th == 0 and e == 1:
                        tb1 = tap("cb1", [8, S])
                        if tb1 is not None:
                            dtk = qG.dma(tb1, combT[0:8, :],
                                         extra_wait=(sS, crt))
                            P.add("s", _w(*dtk))
                        tc1 = tap("cr1", [128, S // 2])
                        if tc1 is not None:
                            dtk = qG.dma(tc1, crep, extra_wait=(sS, crt))
                            P.add("s", _w(*dtk))
                        te1 = tap("ze1", [D, S])
                        if te1 is not None:
                            ze1t = sV.n
                            for och in range(DCH):
                                qG.dma(te1[och * 128:(och + 1) * 128, :],
                                       z[och], extra_wait=(sV, ze1t))
                        th1 = tap("he1", [HM, S // 2])
                        if th1 is not None:
                            for m in range(MCH):
                                qG.dma(th1[m * 128:(m + 1) * 128, :], h_bf[m],
                                       extra_wait=(sV, h_t))
                    if th == 0 and e == NE - 1:
                        te7 = tap("ze7", [D, S])
                        if te7 is not None:
                            ze7t = sV.n
                            for och in range(DCH):
                                qG.dma(te7[och * 128:(och + 1) * 128, :],
                                       z[och], extra_wait=(sV, ze7t))
                    if th == 0 and e == 0:
                        te0 = tap("ze0", [D, S])
                        if te0 is not None:
                            ze0t = sV.n
                            for och in range(DCH):
                                qG.dma(te0[och * 128:(och + 1) * 128, :],
                                       z[och], extra_wait=(sV, ze0t))
                        th0 = tap("he0", [HM, S // 2])
                        if th0 is not None:
                            for m in range(MCH):
                                qG.dma(th0[m * 128:(m + 1) * 128, :], h_bf[m],
                                       extra_wait=(sV, h_t))
                        tc0 = tap("cr0", [128, S // 2])
                        if tc0 is not None:
                            dtk = qG.dma(tc0, crep, extra_wait=(sS, crt))
                            P.add("s", _w(*dtk))
            z2_t = sV.n
            t_z2 = tap("z2", [D, S])
            if t_z2 is not None:
                for och in range(DCH):
                    qG.dma(t_z2[och * 128:(och + 1) * 128, :], z[och],
                           extra_wait=(sV, z2_t))
            chunk_barrier()

            AR.release(zend)
            shf = [AR.b(512) for _ in range(2)]
            shf2 = [AR.b(512) for _ in range(2)]
            shf_fr = [0, 0]
            m2rep = AR.f(S)
            r2rep = AR.f(S)
            x1rl2 = [AR.f(S) for _ in range(2)]
            x1_fr = [0, 0]
            x2b = [AR.f(S) for _ in range(2)]
            x2_fr = [None, None]
            rows3B = AR.b(S)
            rows3 = [rows3B[0:1, :], rows3B[32:33, :]]
            rowsF2 = AR.f(S)
            mt2 = rowsF2[0:1, 0:S]
            vt2 = rowsF2[32:33, 0:S]
            rt2 = rowsF2[64:65, 0:S]
            q2f = m2rep[32:33, 0:S]
            P.add("v", _memset(sumc, 0.0))
            P.add("v", _memset(sumc2, 0.0, sem=sV))
            zs_t = sV.n
            first = True
            for och in range(DCH):
                for i in range(4):
                    isl = slice(i * 512, (i + 1) * 512)
                    sp = i % 2
                    if shf_fr[sp]:
                        P.add("s", _w(sT, shf_fr[sp]))
                    P.add("s", _act(shf[sp], z[och][:, isl], AF.Copy))
                    P.add("s", _act(shf2[sp], z[och][:, isl], AF.Square,
                                    sem=sS))
                    ct = sS.n
                    if first:
                        banks.twait(4)
                        banks.twait(5)
                        P.add("t", _w(sV, zs_t))
                        first = False
                    else:
                        P.add("t", _w(sV, sV.n))
                    P.add("t", _w(sS, ct))
                    P.add("t", _mm(ps[4][0:1, :], ones_col, shf[sp],
                                   True, True))
                    P.add("t", _mm(ps[5][0:1, :], ones_col, shf2[sp],
                                   True, True, sem=sT))
                    shf_fr[sp] = sT.n
                    P.add("v", _w(sT, sT.n))
                    P.add("v", _tt(sumc[:, isl], sumc[:, isl], ps[4][0:1, :],
                                   OP.add))
                    P.add("v", _tt(sumc2[:, isl], sumc2[:, isl],
                                   ps[5][0:1, :], OP.add, sem=sV))
            banks.set_free(4, sV, sV.n)
            banks.set_free(5, sV, sV.n)
            P.add("v", _ts(mt2, sumc, 1.0 / D, OP.mult))
            P.add("v", _ts(vt2, sumc2, 1.0 / D, OP.mult))
            P.add("v", _tt(q2f, mt2, mt2, OP.mult))
            P.add("v", _tt(vt2, vt2, q2f, OP.subtract, sem=sV))
            P.add("s", _w(sV, sV.n))
            P.add("s", _act(vt2, vt2, AF.Sqrt, bias=c_eps[32:33, 0:1], sem=sS))
            P.add("v", _w(sS, sS.n))
            P.add("v", _recip(rt2, vt2))
            P.add("v", _memset(srow[0:1, 13:14], 0.0, sem=sV))
            P.add("s", _w(sV, sV.n))
            P.add("s", _act(rows3[0], mt2, AF.Copy))
            P.add("s", _act(rows3[1], rt2, AF.Copy, sem=sS))
            P.add("t", _w(sS, sS.n))
            replicate_row(rows3[0], m2rep, 4)
            replicate_row(rows3[1], r2rep, 5, bp=32)
            frep_t = sS.n
            P.add("v", _w(sS, frep_t))
            for och in range(DCH):
                sp = och % 2
                ew = (sV, x1_fr[sp]) if x1_fr[sp] else None
                xtk = qG.dma(x1rl2[sp], x1sp[och * 128:(och + 1) * 128, :],
                             extra_wait=ew)
                P.add("v", _w(*xtk))
                if x2_fr[sp] is not None:
                    P.add("v", _w(*x2_fr[sp]))
                P.add("v", _tt(x2b[sp], z[och], m2rep, OP.subtract))
                P.add("v", _tt(x2b[sp], x2b[sp], r2rep, OP.mult))
                P.add("v", _tt(x2b[sp], x1rl2[sp], x2b[sp], OP.add, sem=sV))
                x1_fr[sp] = sV.n
                x2_fr[sp] = qG.dma(x2T[och * 128:(och + 1) * 128, :],
                                   x2b[sp], extra_wait=(sV, sV.n))

        chunk_barrier()
        qG.drain()
        qW.drain()

        with nc.Block() as block:
            @block.sync
            def _(eng):
                for f in P.ops["sy"]:
                    f(eng)

            @block.gpsimd
            def _(eng):
                for f in P.ops["g"]:
                    f(eng)

            @block.scalar
            def _(eng):
                for f in P.ops["s"]:
                    f(eng)

            @block.vector
            def _(eng):
                for f in P.ops["v"]:
                    f(eng)

            @block.tensor
            def _(eng):
                for f in P.ops["t"]:
                    f(eng)

    return nc


def emit_phase2(env):
    raise NotImplementedError


_CACHE = {}


def _prep_shared(i):
    asc = np.ascontiguousarray
    sh = {}
    sh["w_inT"] = asc(i["L0_attn_in_proj_w"].T)
    cw = i["L0_attn_conv_w"].reshape(DI, 4)
    sh["convw"] = asc(cw.reshape(16, 128, 4).transpose(1, 0, 2).reshape(128, 64))
    sh["convb"] = asc(i["L0_attn_conv_b"].reshape(16, 128).T)
    xp = np.zeros((DI, 128), np.float32)
    xp[:, 0:64] = i["L0_attn_x_proj_w"].T[:, 0:64]
    xp[:, 64:80] = i["L0_attn_x_proj_w"].T[:, 64:80]
    xp[:, 96:112] = i["L0_attn_x_proj_w"].T[:, 80:96]
    sh["xprojT"] = xp
    sh["dtprojT"] = asc(i["L0_attn_dt_proj_w"].T)
    sh["dtbd"] = asc(i["L0_attn_dt_proj_b"].reshape(16, 128).T)
    sh["alogd"] = asc(i["L0_attn_A_log"].reshape(16, 128, 16)
                      .transpose(1, 0, 2).reshape(128, 256))
    sh["ddev"] = asc(i["L0_attn_D"].reshape(16, 128).T)
    sh["outpT"] = asc(i["L0_attn_out_proj_w"].T)
    sh["f1T"] = asc(i["L0_attn_ffn_w1"].T)
    sh["fb1d"] = asc(i["L0_attn_ffn_b1"].reshape(32, 128).T)
    sh["f2T"] = asc(i["L0_attn_ffn_w2"].T)
    sh["fb2d"] = asc(i["L0_attn_ffn_b2"].reshape(8, 128).T)
    sh["n1wd"] = asc(i["L0_ffn_norm1_w"].reshape(8, 128).T)
    sh["n2wd"] = asc(i["L0_ffn_norm2_w"].reshape(8, 128).T)
    sh["qkvT"] = asc(i["L0_ffn_qkv_w"].T)
    sh["qkvbd"] = asc(i["L0_ffn_qkv_b"].reshape(24, 128).T)
    sh["bvrow"] = asc(i["L0_ffn_qkv_b"][2048:3072].reshape(1, D))
    sh["outwT"] = asc(i["L0_ffn_out_w"].T)
    sh["outbd"] = asc(i["L0_ffn_out_b"].reshape(8, 128).T)
    sh["gwT"] = asc(i["L0_ffn_gate_w"].T)
    sh["w1T"] = asc(i["L0_ffn_w1"].transpose(0, 2, 1))
    sh["w2T"] = asc(i["L0_ffn_w2"].transpose(0, 2, 1))
    sh["w3T"] = asc(i["L0_ffn_w3"].transpose(0, 2, 1))
    sh["b1d"] = asc(i["L0_ffn_b1"].reshape(NE, 16, 128)
                    .transpose(2, 0, 1).reshape(128, 128))
    sh["b2d"] = asc(i["L0_ffn_b2"].reshape(NE, 16, 128)
                    .transpose(2, 0, 1).reshape(128, 128))
    sh["b3d"] = asc(i["L0_ffn_b3"].reshape(NE, 8, 128)
                    .transpose(2, 0, 1).reshape(128, 64))
    sh["identb"] = np.eye(128, dtype=B16)
    sh["onesb"] = np.ones((128, 128), dtype=B16)
    rp = np.zeros((128, 128), np.float32)
    for r in range(128):
        cc = r % 64
        base = r - cc
        if cc < 32:
            rp[base + cc + 32, r] = -1.0
        else:
            rp[base + cc - 32, r] = 1.0
    sh["rotP"] = rp.astype(B16)
    invf = 1.0 / (10000.0 ** (np.arange(0, 64, 2, dtype=np.float64) / 64.0))
    t = np.arange(S, dtype=np.float64)
    fr = t[:, None] * invf[None, :]
    emb = np.concatenate([fr, fr], axis=1)
    sc = 8.0 ** -0.5
    sh["cosT"] = np.tile((np.cos(emb) * sc).T, (2, 1)).astype(np.float32).astype(B16)
    sh["sinT"] = np.tile((np.sin(emb) * sc).T, (2, 1)).astype(np.float32).astype(B16)
    sel = np.zeros((16, 16, 128), np.float32)
    for n in range(16):
        sel[n, n, :] = 1.0
    sh["selb"] = sel.transpose(1, 0, 2).reshape(16, 2048).astype(B16)
    hbq = np.zeros((128, 4, 8), np.float32)
    for i in range(4):
        for p in range(128):
            hbq[p, i, 2 * i + p // 64] = 1.0
    sh["hbq"] = hbq.reshape(128, 32).astype(B16)
    return sh


def _run(inputs, upto=None, taps=None, trace=False):
    upto = upto if upto is not None else _OPTS["upto"]
    taps = tuple(taps if taps is not None else _OPTS["taps"])
    key = (upto, taps)
    if key not in _CACHE:
        _CACHE[key] = build(upto=upto, taps=taps)
    nc = _CACHE[key]
    sh = _prep_shared({k: np.asarray(v) for k, v in inputs.items()})
    x = np.asarray(inputs["L0_attn_seq_embeddings"])
    in_maps = []
    for c in range(8):
        m = dict(sh)
        m["x0T"] = np.ascontiguousarray(x[c].T)
        in_maps.append(m)
    return run_bass_kernel_spmd(nc, in_maps, core_ids=list(range(8)),
                                trace=trace)


def kernel(**inputs):
    res = _run(inputs)
    out = np.stack([np.asarray(r["x2T"]).T for r in res.results], axis=0)
    return out.astype(np.float32)
